# version 12
# baseline (speedup 1.0000x reference)
import numpy as np
import ml_dtypes
from contextlib import ExitStack

import concourse.bass as bass
import concourse.mybir as mybir
from concourse.bass_utils import run_bass_kernel_spmd

F32 = mybir.dt.float32
BF16 = mybir.dt.bfloat16
AF = mybir.ActivationFunctionType
ALU = mybir.AluOpType
AX = mybir.AxisListType

D = 1024
KD = 8
E = 8
I2 = 4096
ICN = 16
LN_EPS = 1e-5
RMS_EPS = 1e-6
PH_EPS = 1e-5


def _align(x, a=64):
    return (x + a - 1) // a * a


class Builder:
    ENGS = ("sy", "gp", "pe", "ac", "dv")
    SEMS = ("sx", "swg", "spe", "sact", "sdve", "sst", "sgc")

    def __init__(self, L, T, C, S=512, debug=False):
        assert T % S == 0 and T % C == 0 and L % T == 0 and T <= 1024
        self.L, self.T, self.C, self.S = L, T, C, S
        self.ntiles = L // T
        self.q = {e: [] for e in self.ENGS}
        self.c = {s: 0 for s in self.SEMS}
        self.rel = {}
        self.sem = {}
        self.debug = debug
        self.paranoid = False
        self.dbg_out = {}

    def tap(self, name, sbuf_ap, dram_slice_fn, t):
        if not self.debug:
            return 0
        dr = self.dbg_out[name]
        self.wait("sy", "sdve", self.c["sdve"])
        self.wait("sy", "sact", self.c["sact"])
        return self.dma("sy", dram_slice_fn(dr, t), sbuf_ap)

    def run(self, eng, fn):
        self.q[eng].append(fn)

    def wait(self, eng, sem_name, val):
        assert val <= self.c[sem_name], (eng, sem_name, val, self.c[sem_name])
        if val <= 0:
            return
        self.q[eng].append(lambda EE, s=sem_name, v=val: EE.wait_ge(self.sem[s], v))

    def inc(self, eng, fn, sem_name="spe" , n=1):
        self.c[sem_name] += n
        v = self.c[sem_name]
        self.q[eng].append(
            lambda EE, f=fn, s=sem_name, k=n: f(EE).then_inc(self.sem[s], k))
        return v

    def dma(self, eng, out_ap, in_ap, sem="sx"):
        return self.inc(eng, lambda EE, o=out_ap, i=in_ap: EE.dma_start(out=o, in_=i),
                        sem, 16)

    def pad_inc(self, eng, sem_name, n=2):
        s = self.s
        for _ in range(n):
            if eng == "dv":
                self.run("dv", lambda E_: E_.tensor_copy(s["padd"][...], s["t8c"][...]))
            else:
                self.run("ac", lambda E_: E_.activation(out=s["pada"][...],
                                                        in_=s["padd"][...], func=AF.Copy))
        if eng == "dv":
            return self.inc("dv", lambda E_: E_.tensor_copy(s["padd"][...], s["t8c"][...]),
                            sem_name)
        return self.inc("ac", lambda E_: E_.activation(out=s["pada"][...],
                                                       in_=s["padd"][...], func=AF.Copy),
                        sem_name)

    def need(self, eng, res):
        r = self.rel.get(res)
        if r:
            self.wait(eng, r[0], r[1])

    def release(self, res, sem_name, val):
        self.rel[res] = (sem_name, val)

    def build(self):
        L, T, C, S = self.L, self.T, self.C, self.S
        nc = bass.Bass("TRN2", target_bir_lowering=False)
        self.nc = nc

        dx = nc.dram_tensor("x0", [128, KD, L], F32, kind="ExternalInput").ap()
        dout = nc.dram_tensor("out", [128, KD, L], F32, kind="ExternalOutput").ap()
        dw = {}
        for n in ("wz", "wza", "wy", "wya"):
            dw[n] = nc.dram_tensor(n, [128, KD, D], F32, kind="ExternalInput").ap()
        for n in ("bz", "bza", "by", "bya", "nw", "lcr", "lci",
                  "f0r", "f0i", "rcr", "rci"):
            dw[n] = nc.dram_tensor(n, [128, KD], F32, kind="ExternalInput").ap()
        for n in ("cr", "sr"):
            dw[n] = nc.dram_tensor(n, [128, KD, C], F32, kind="ExternalInput").ap()
        dw["w1"] = nc.dram_tensor("w1", [E, 128, KD, I2], F32, kind="ExternalInput").ap()
        dw["w2"] = nc.dram_tensor("w2", [E, 128, ICN, D], F32, kind="ExternalInput").ap()
        dw["rwt"] = nc.dram_tensor("rwt", [128, KD, E], F32, kind="ExternalInput").ap()
        dw["idn"] = nc.dram_tensor("idn", [128, 128], F32, kind="ExternalInput").ap()
        dw["onesr"] = nc.dram_tensor("onesr", [1, 128], F32, kind="ExternalInput").ap()
        dw["onesrb"] = nc.dram_tensor("onesrb", [1, 128], BF16, kind="ExternalInput").ap()
        dw["onesc"] = nc.dram_tensor("onesc", [128, 1], F32, kind="ExternalInput").ap()
        dw["onesc2"] = nc.dram_tensor("onesc2", [128, 1], BF16, kind="ExternalInput").ap()
        self.dx, self.dout, self.dw = dx, dout, dw
        if self.debug:
            for n, shp, dt in (("d_xn", [128, KD, L], BF16), ("d_z", [128, KD, L], F32),
                               ("d_h", [128, KD, L], F32), ("d_x1", [128, KD, L], F32),
                               ("d_gt", [128, (L // 128) * 8], F32), ("d_ew", [8, L], BF16),
                               ("d_xn2", [128, KD, L], BF16), ("d_g", [128, ICN, L], BF16)):
                self.dbg_out[n] = nc.dram_tensor(n, shp, dt, kind="ExternalOutput").ap()
            self.dbg_out["d_ini"] = nc.dram_tensor("d_ini", [128, 64], F32,
                                                   kind="ExternalOutput").ap()

        off = [16512]

        def sb(name, shape, dtype, at=None):
            sz = int(np.prod(shape[1:])) * (4 if dtype == F32 else 2)
            if at is None:
                at = off[0]
                off[0] = _align(at + sz)
            return nc.alloc_sbuf_tensor_at(name, list(shape), dtype, offset=at)

        s = {}
        s["x0"] = sb("x0s", [128, KD, T], F32)
        xn_base = off[0]
        s["xn"] = sb("xns", [128, KD, T], BF16)
        zh_base = off[0]
        s["zh"] = sb("zhs", [128, KD, T], F32)
        ps_bytes = 2 * KD * D
        self.ps = [sb(f"ps{i}", [128, KD, D], BF16, at=zh_base + i * ps_bytes)
                   for i in range(3)]
        off[0] = max(off[0], _align(zh_base + 3 * ps_bytes))
        dg_base = off[0]
        s["dgate"] = sb("dgate", [128, KD, T], BF16)
        e_base = off[0]
        s["g"] = sb("gts", [128, ICN, T], BF16)
        self.ws = [sb(f"ws{i}", [128, KD, D], BF16, at=e_base + i * ps_bytes)
                   for i in range(2)]
        off[0] = max(off[0], _align(e_base + max(2 * ps_bytes, 2 * ICN * T)))
        for n in ("cr", "sr"):
            s[n] = sb(n + "s", [128, KD, C], F32)
        s["szb"] = sb("szb", [128, S], F32)
        s["szb2"] = sb("szb2", [128, S], F32)
        s["szbb"] = sb("szbb", [128, S], BF16)
        s["szb2b"] = sb("szb2b", [128, S], BF16)
        for n in ("u_r", "u_i", "c_r", "c_i", "m_r", "m_i"):
            s[n] = sb(n, [128, 2, C], F32)
        s["rb"] = sb("rbs", [128, T], F32)
        s["ab"] = sb("abs", [128, T], F32)
        s["ewb"] = sb("ewbs", [128, T], BF16)
        s["rows"] = sb("rows", [1, 3 * S], F32)
        s["rows_b"] = sb("rows_b", [1, 2 * S], BF16)
        s["ew_fm"] = sb("ew_fm", [8, T], BF16)
        s["ew_row"] = sb("ew_row", [1, T], BF16)
        ngrp = T // 128
        s["gt"] = sb("gt", [128, ngrp * 8], F32)
        s["ex"] = sb("ex", [128, ngrp * 8], F32)
        s["mskb"] = sb("mskb", [128, ngrp * 8], F32)
        s["cnts"] = sb("cnts", [128, ngrp * 8], F32)
        s["junk8"] = sb("junk8", [128, 8], F32)
        s["cnt8"] = sb("cnt8", [128, 8], F32)
        s["mx"] = sb("mx", [128, 1], F32)
        s["sm"] = sb("sm", [128, 1], F32)
        s["rc"] = sb("rc", [128, 1], F32)
        s["ewsb"] = sb("ewsb", [128, 8], F32)
        s["q_r"] = sb("q_r", [128, 2, KD], F32)
        s["q_i"] = sb("q_i", [128, 2, KD], F32)
        for n in ("bz", "bza", "by", "bya", "nw", "lcr", "lci",
                  "f0r", "f0i", "rcr", "rci"):
            s[n] = sb(n + "s", [128, KD], F32)
        s["rwt"] = sb("rwts", [128, KD, E], F32)
        s["idn"] = sb("idns", [128, 128], F32)
        s["onesr"] = sb("onesrs", [1, 128], F32)
        s["onesrb"] = sb("onesrbs", [1, 128], BF16)
        s["onesc"] = sb("onescs", [128, 1], F32)
        s["onesc2"] = sb("onesc2s", [128, 1], BF16)
        s["ones_sc"] = sb("ones_sc", [128, C], F32)
        s["t8a"] = sb("t8a", [128, 8], F32)
        s["t8b"] = sb("t8b", [128, 8], F32)
        s["t8c"] = sb("t8c", [128, 8], F32)
        s["t8d"] = sb("t8d", [128, 8], F32)
        s["padd"] = sb("padd", [128, 8], F32)
        s["pada"] = sb("pada", [128, 8], F32)
        s["mxs"] = sb("mxs", [128, T // 128], F32)
        s["sms"] = sb("sms", [128, T // 128], F32)
        assert off[0] <= 229344, off[0]
        self.s = s
        self.pk = [nc.alloc_psum_tensor(f"pk{i}", [128, 512], F32) for i in range(8)]
        self.slot_t = {"ps0": self.ps[0], "ps1": self.ps[1], "ps2": self.ps[2],
                       "ws0": self.ws[0], "ws1": self.ws[1]}
        self.slot_busy = set()

        with ExitStack() as ctx:
            for sm in self.SEMS:
                self.sem[sm] = ctx.enter_context(nc.semaphore(sm))
            self.schedule()
            with nc.Block() as block:
                @block.sync
                def _(eng):
                    for fn in self.q["sy"]:
                        fn(eng)

                @block.gpsimd
                def _(eng):
                    for fn in self.q["gp"]:
                        fn(eng)

                @block.tensor
                def _(eng):
                    for fn in self.q["pe"]:
                        fn(eng)

                @block.scalar
                def _(eng):
                    for fn in self.q["ac"]:
                        fn(eng)

                @block.vector
                def _(eng):
                    for fn in self.q["dv"]:
                        fn(eng)
        return nc

    def schedule(self):
        s = self.s
        T = self.T

        for n in ("onesc2", "onesrb", "onesc", "onesr"):
            self.dma("sy", s[n][...], self.dw[n])
        nsub0 = T // self.S
        self.xload = [[("sx", self.dma("sy", s["x0"][:, :, u * self.S:(u + 1) * self.S],
                                       self.dx[:, :, u * self.S:(u + 1) * self.S]))
                       for u in range(nsub0)]]
        for n in ("bz", "bza", "by", "bya", "nw", "lcr", "lci",
                  "f0r", "f0i", "rcr", "rci", "rwt", "idn", "cr", "sr"):
            self.dma("sy", s[n][...], self.dw[n])
        self.init_sx = self.c["sx"]

        self.w_att = {}
        self.w_att["za"] = self.load_slot("ws0", self.dw["wza"])
        self.w_att["z"] = self.load_slot("ws1", self.dw["wz"])
        self.w_att["ya"] = self.load_slot("ps2", self.dw["wya"])

        self.run("dv", lambda E_: E_.memset(s["ones_sc"][...], float(np.exp(PH_EPS))))
        self.wait("dv", "sx", self.init_sx)
        self.run("dv", lambda E_: E_.tensor_mul(s["t8a"][...], s["lcr"][...], s["f0r"][...]))
        self.run("dv", lambda E_: E_.tensor_mul(s["t8b"][...], s["lci"][...], s["f0i"][...]))
        self.run("dv", lambda E_: E_.tensor_mul(s["t8c"][...], s["lcr"][...], s["f0i"][...]))
        self.run("dv", lambda E_: E_.tensor_mul(s["t8d"][...], s["lci"][...], s["f0r"][...]))
        self.run("dv", lambda E_: E_.tensor_sub(s["q_r"][:, 0, :], s["t8a"][...], s["t8b"][...]))
        self.run("dv", lambda E_: E_.tensor_add(s["q_i"][:, 0, :], s["t8c"][...], s["t8d"][...]))
        self.run("dv", lambda E_: E_.tensor_copy(s["padd"][...], s["t8c"][...]))
        self.run("dv", lambda E_: E_.tensor_copy(s["padd"][...], s["t8c"][...]))
        self.run("dv", lambda E_: E_.tensor_scalar_mul(s["q_i"][:, 0, :], s["q_i"][:, 0, :], -1.0))
        if self.debug:
            d = self.inc("dv", lambda E_: E_.tensor_copy(s["junk8"][...], s["q_i"][:, 0, :]),
                         "sdve")
            self.wait("sy", "sdve", d)
            dr = self.dbg_out["d_ini"]
            for i, nm in enumerate(("lcr", "lci", "f0r", "f0i", "rcr", "rci")):
                self.dma("sy", dr[:, i * 8:(i + 1) * 8], s[nm][...])
            tc = self.dma("sy", dr[:, 48:56], s["q_r"][:, 0, :])
            tc = self.dma("sy", dr[:, 56:64], s["q_i"][:, 0, :])
            self.wait("dv", "sx", tc)

        self.nstores = 0
        self.gu_loaded = {}
        self.w2_loaded = {}
        for t in range(self.ntiles):
            self.tile(t)
        self.wait("sy", "sst", 16 * self.nstores)

    def layernorm(self, src, want_f32, extra_dv=0, xb_guard=0):
        s, pk = self.s, self.pk
        self.xb_guard = xb_guard
        T, S = self.T, self.S
        for sub in range(T // S):
            tok = slice(sub * S, (sub + 1) * S)
            for k in range(KD):
                if k == 0:
                    self.wait("dv", *self.xsrc_ready[sub])
                    if sub == 0 and extra_dv:
                        self.wait("dv", "sdve", extra_dv)
                    if self.xb_guard:
                        self.wait("dv", "spe", self.xb_guard)
                fn = (lambda E_, k=k, tok=tok: E_.tensor_copy(
                    s["xn"][:, k, tok], src[:, k, tok]))
                if k == KD - 1:
                    xb_d = self.inc("dv", fn, "sdve")
                else:
                    self.run("dv", fn)
            for k in range(KD):
                sq = ("szbb", "szb2b")[k % 2]
                self.need("ac", sq)
                if k == 0:
                    self.wait("ac", "sdve", xb_d)
                a = self.inc("ac", lambda E_, k=k, sq=sq, tok=tok: E_.activation(
                    out=s[sq][...], in_=s["xn"][:, k, tok],
                    func=AF.Square), "sact")
                if k == 0:
                    self.need("pe", "bank6")
                    self.need("pe", "bank7")
                    self.wait("pe", "sdve", xb_d)
                self.wait("pe", "sact", a)
                p = self.inc("pe", lambda E_, k=k, sq=sq: E_.matmul(
                    pk[6][0:1, 0:S], s["onesc2"][...], s[sq][...],
                    start=(k == 0), stop=(k == KD - 1)), "spe")
                self.release(sq, "spe", p)
                p = self.inc("pe", lambda E_, k=k, tok=tok: E_.matmul(
                    pk[7][0:1, 0:S], s["onesc2"][...], s["xn"][:, k, tok],
                    start=(k == 0), stop=(k == KD - 1)), "spe")
            stats_done = p
            rm = s["rows"][0:1, 0:S]
            rv = s["rows"][0:1, S:2 * S]
            rcx = s["rows"][0:1, 2 * S:3 * S]
            rvb = s["rows_b"][0:1, 0:S]
            rab = s["rows_b"][0:1, S:2 * S]
            self.need("dv", "row_bufs")
            self.wait("dv", "spe", stats_done)
            self.run("dv", lambda E_, rm=rm: E_.tensor_scalar_mul(rm, pk[7][0:1, 0:S], 1.0 / D))
            self.run("dv", lambda E_, rv=rv: E_.tensor_scalar(
                rv, pk[6][0:1, 0:S], 1.0 / D, LN_EPS, ALU.mult, ALU.add))
            self.run("dv", lambda E_, rm=rm, rcx=rcx: E_.tensor_mul(rcx, rm, rm))
            d = self.inc("dv", lambda E_, rv=rv, rcx=rcx: E_.tensor_sub(rv, rv, rcx), "sdve")
            self.release("bank6", "sdve", d)
            self.release("bank7", "sdve", d)
            self.wait("ac", "sdve", d)
            self.run("ac", lambda E_, rv=rv: E_.activation(out=rv, in_=rv, func=AF.Ln))
            a = self.inc("ac", lambda E_, rv=rv, rvb=rvb: E_.activation(
                out=rvb, in_=rv, func=AF.Exp, scale=-0.5), "sact")
            self.wait("dv", "sact", a)
            d = self.inc("dv", lambda E_, rm=rm, rvb=rvb, rab=rab: E_.tensor_mul(
                rab, rm, rvb), "sdve")
            self.need("pe", "bank6")
            self.need("pe", "bank7")
            self.wait("pe", "sdve", d)
            self.inc("pe", lambda E_, rvb=rvb: E_.matmul(
                pk[6][:, 0:S], s["onesrb"][...], rvb, start=True, stop=True), "spe")
            p = self.inc("pe", lambda E_, rab=rab: E_.matmul(
                pk[7][:, 0:S], s["onesrb"][...], rab, start=True, stop=True), "spe")
            self.release("row_bufs", "spe", p)
            self.wait("dv", "spe", p)
            self.run("dv", lambda E_, tok=tok: E_.tensor_copy(s["rb"][:, tok], pk[6][:, 0:S]))
            d = self.inc("dv", lambda E_, tok=tok: E_.tensor_copy(s["ab"][:, tok], pk[7][:, 0:S]),
                         "sdve")
            self.release("bank6", "sdve", d)
            self.release("bank7", "sdve", d)
            for k in range(KD):
                self.run("dv", lambda E_, k=k, tok=tok: E_.tensor_mul(
                    s["szb"][...], src[:, k, tok], s["rb"][:, tok]))
                if want_f32:
                    self.run("dv", lambda E_, k=k, tok=tok: E_.tensor_sub(
                        s["zh"][:, k, tok], s["szb"][...], s["ab"][:, tok]))
                    fn = (lambda E_, k=k, tok=tok: E_.tensor_copy(
                        s["xn"][:, k, tok], s["zh"][:, k, tok]))
                else:
                    fn = (lambda E_, k=k, tok=tok: E_.tensor_sub(
                        s["xn"][:, k, tok], s["szb"][...], s["ab"][:, tok]))
                if k == KD - 1:
                    self.ln_done[sub] = self.inc("dv", fn, "sdve")
                else:
                    self.run("dv", fn)
            self.release("szb", "sdve", self.ln_done[sub])

    def proj_pass(self, kind, sub, banks):
        s, pk = self.s, self.pk
        S = self.S
        tok = slice(sub * S, (sub + 1) * S)
        wslot, wcnt = self.w_att[kind]
        w = self.slot_t[wslot]
        for m in range(KD):
            b = banks[m % len(banks)]
            self.need("pe", f"bank{b}")
            if m == 0 and sub == 0:
                self.wait("pe", "swg", wcnt)
            if m == 0:
                self.wait("pe", "sdve", self.ln_done[sub])
            for k in range(KD):
                fn = (lambda E_, k=k, m=m, b=b, tok=tok: E_.matmul(
                    pk[b][...], w[:, k, m * 128:(m + 1) * 128], s["xn"][:, k, tok],
                    start=(k == 0), stop=(k == KD - 1)))
                if k == KD - 1:
                    p = self.inc("pe", fn, "spe")
                else:
                    self.run("pe", fn)
            if kind == "za":
                self.wait("ac", "spe", p)
                a = self.inc("ac", lambda E_, m=m, b=b, tok=tok: E_.activation(
                    out=s["zh"][:, m, tok], in_=pk[b][...], func=AF.Silu,
                    bias=s["bza"][:, m:m + 1]), "sact")
                self.release(f"bank{b}", "sact", a)
                self.za_done = a
            elif kind == "z":
                self.wait("dv", "spe", p)
                self.wait("dv", "sact", self.za_done)
                d = self.inc("dv", lambda E_, m=m, b=b, tok=tok: E_.scalar_tensor_tensor(
                    s["zh"][:, m, tok], pk[b][...], s["bz"][:, m:m + 1],
                    s["zh"][:, m, tok], ALU.add, ALU.mult), "sdve")
                self.release(f"bank{b}", "sdve", d)
                self.z_done_sub[sub] = d
            elif kind == "ya":
                self.wait("ac", "spe", p)
                a = self.inc("ac", lambda E_, m=m, b=b, tok=tok: E_.activation(
                    out=s["dgate"][:, m, tok], in_=pk[b][...], func=AF.Silu,
                    bias=s["bya"][:, m:m + 1]), "sact")
                self.release(f"bank{b}", "sact", a)
                self.ya_done = a
            elif kind == "wy":
                self.wait("dv", "spe", p)
                if m == 0 and sub == 0:
                    self.wait("dv", "sact", self.ya_done)
                self.run("dv", lambda E_, m=m, b=b, tok=tok: E_.scalar_tensor_tensor(
                    s["szb"][...], pk[b][...], s["by"][:, m:m + 1],
                    s["dgate"][:, m, tok], ALU.add, ALU.mult))
                d = self.inc("dv", lambda E_, m=m, tok=tok: E_.tensor_add(
                    s["x0"][:, m, tok], s["x0"][:, m, tok], s["szb"][...]), "sdve")
                self.release(f"bank{b}", "sdve", d)
                self.release("szb", "sdve", d)
                self.wy_done = d

    def load_slot(self, name, dram_ap):
        self.need("gp", name)
        v = self.dma("gp", self.slot_t[name][...], dram_ap, "swg")
        self.slot_busy.add(name)
        return (name, v)

    def free_slot(self, name, pe_val):
        self.release(name, "spe", pe_val)
        self.slot_busy.discard(name)

    def pick_ps(self):
        best, bc = None, None
        for n in ("ps0", "ps1", "ps2"):
            if n in self.slot_busy:
                continue
            c = self.rel.get(n, ("spe", 0))[1]
            if bc is None or c < bc:
                best, bc = n, c
        assert best is not None, "no free ps slot"
        return best

    def prefetch_gu(self, key, e, j, slot=None):
        if slot is None:
            slot = self.pick_ps()
        self.gu_loaded[key] = self.load_slot(
            slot, self.dw["w1"][e, :, :, j * 1024:(j + 1) * 1024])

    def prefetch_w2(self, e, half, slot=None):
        if slot is None:
            slot = self.pick_ps()
        self.w2_loaded[(e, half)] = self.load_slot(
            slot, self.dw["w2"][e, :, half * 8:(half + 1) * 8, :])

    def tile(self, t):
        s, pk = self.s, self.pk
        T, C, S = self.T, self.C, self.S
        nsub = T // S
        ncc = T // C
        ngrp = T // 128
        self.ln_done = [0] * nsub
        self.z_done_sub = [0] * nsub

        self.xsrc_ready = self.xload[t]
        self.layernorm(s["x0"], want_f32=False,
                       xb_guard=getattr(self, "xn_free_pe", 0) or self.c["spe"])
        tc = self.tap("d_xn", s["xn"][...], lambda dr, t=t: dr[:, :, t * T:(t + 1) * T], t)
        if tc:
            self.wait("dv", "sx", tc)

        for sub in range(nsub):
            self.proj_pass("za", sub, (0, 2, 4))
            self.proj_pass("z", sub, (1, 3, 5))
        tc = self.tap("d_z", s["zh"][...], lambda dr, t=t: dr[:, :, t * T:(t + 1) * T], t)
        if tc:
            self.wait("dv", "sx", tc)
        zaz_pe = self.c["spe"]
        self.free_slot("ws0", zaz_pe)
        self.free_slot("ws1", zaz_pe)

        for sub in range(nsub):
            self.proj_pass("ya", sub, (0, 1, 2, 3, 4, 5))
        self.free_slot("ps2", self.c["spe"])

        groups = [(cc, k) for cc in range(ncc) for k in range(KD)]
        last_zwait = 0
        NG = len(groups)
        h_cnt = {}
        scan_ms = {}
        m_ms = {}
        u_ms = {}
        for gi in range(NG + 1):
            if gi < NG:
                cc, k = groups[gi]
                p2 = gi % 2
                ck = slice(cc * C, (cc + 1) * C)
                zneed = self.z_done_sub[(cc * C) // self.S]
                if zneed > last_zwait:
                    self.wait("gp", "sdve", zneed)
                    last_zwait = zneed
                if gi >= 2:
                    self.wait("gp", "sdve", scan_ms[gi - 2])
                self.run("gp", lambda E_, k=k, ck=ck, p2=p2: E_.tensor_mul(
                    s["u_r"][:, p2, :], s["zh"][:, k, ck], s["cr"][:, k, :]))
                u_ms[gi] = self.inc("gp", lambda E_, k=k, ck=ck, p2=p2: E_.tensor_mul(
                    s["u_i"][:, p2, :], s["zh"][:, k, ck], s["sr"][:, k, :]), "sgc")
            if gi >= 1:
                pkk = groups[gi - 1][1]
                pp = (gi - 1) % 2
                self.wait("gp", "sdve", scan_ms[gi - 1])
                if gi >= 3:
                    self.wait("gp", "sdve", h_cnt[groups[gi - 3]])
                self.run("gp", lambda E_, pkk=pkk, pp=pp: E_.tensor_mul(
                    s["m_r"][:, pp, :], s["cr"][:, pkk, :], s["c_r"][:, pp, :]))
                m_ms[gi - 1] = self.inc("gp", lambda E_, pkk=pkk, pp=pp: E_.tensor_mul(
                    s["m_i"][:, pp, :], s["sr"][:, pkk, :], s["c_i"][:, pp, :]), "sgc")
            if gi < NG:
                cc, k = groups[gi]
                p2 = gi % 2
                par = (t * ncc + cc) % 2
                self.wait("dv", "sgc", u_ms[gi])
                if gi >= 2:
                    self.wait("dv", "sgc", m_ms[gi - 2])
                self.run("dv", lambda E_, k=k, p2=p2, par=par: E_.tensor_tensor_scan(
                    s["c_r"][:, p2, :], s["ones_sc"][...], s["u_r"][:, p2, :],
                    s["q_r"][:, par, k:k + 1], ALU.mult, ALU.add))
                scan_ms[gi] = self.inc("dv", lambda E_, k=k, p2=p2, par=par:
                                       E_.tensor_tensor_scan(
                    s["c_i"][:, p2, :], s["ones_sc"][...], s["u_i"][:, p2, :],
                    s["q_i"][:, par, k:k + 1], ALU.mult, ALU.add), "sdve")
                self.run("dv", lambda E_, k=k, p2=p2: E_.tensor_scalar(
                    s["t8a"][:, 0:1], s["c_i"][:, p2, C - 1:C], s["rci"][:, k:k + 1],
                    None, ALU.mult))
                self.run("dv", lambda E_, k=k, p2=p2: E_.tensor_scalar(
                    s["t8b"][:, 0:1], s["c_r"][:, p2, C - 1:C], s["rci"][:, k:k + 1],
                    None, ALU.mult))
                self.run("dv", lambda E_, k=k, p2=p2, par=par: E_.scalar_tensor_tensor(
                    s["q_r"][:, 1 - par, k:k + 1], s["c_r"][:, p2, C - 1:C],
                    s["rcr"][:, k:k + 1], s["t8a"][:, 0:1], ALU.mult, ALU.add))
                self.run("dv", lambda E_, k=k, p2=p2, par=par: E_.scalar_tensor_tensor(
                    s["q_i"][:, 1 - par, k:k + 1], s["c_i"][:, p2, C - 1:C],
                    s["rcr"][:, k:k + 1], s["t8b"][:, 0:1], ALU.mult, ALU.subtract))
            if gi >= 1:
                pcc, pkk = groups[gi - 1]
                pp = (gi - 1) % 2
                pck = slice(pcc * C, (pcc + 1) * C)
                self.wait("dv", "sgc", m_ms[gi - 1])
                h_cnt[groups[gi - 1]] = self.inc(
                    "dv", lambda E_, pkk=pkk, pck=pck, pp=pp: E_.tensor_add(
                        s["zh"][:, pkk, pck], s["m_r"][:, pp, :], s["m_i"][:, pp, :]),
                    "sdve")

        self.w_att["wy"] = self.load_slot("ws0", self.dw["wy"])
        self.prefetch_gu((0, 0), 0, 0, slot="ps2")

        tc = self.tap("d_h", s["zh"][...], lambda dr, t=t: dr[:, :, t * T:(t + 1) * T], t)
        if tc:
            self.wait("dv", "sx", tc)

        qn = 512 // C
        rms_half_done = []
        for half in range((T + 511) // 512):
            bnk = 6 + half
            for k in range(KD):
                sq = ("szbb", "szb2b")[k % 2]
                self.need("ac", sq)
                for q4 in range(qn):
                    cc = half * qn + q4
                    self.wait("ac", "sdve", h_cnt[(cc, k)])
                    a = self.inc("ac", lambda E_, k=k, sq=sq, q4=q4, cc=cc: E_.activation(
                        out=s[sq][:, q4 * C:(q4 + 1) * C],
                        in_=s["zh"][:, k, cc * C:(cc + 1) * C],
                        func=AF.Square), "sact")
                if k == 0:
                    self.need("pe", f"bank{bnk}")
                self.wait("pe", "sact", a)
                p = self.inc("pe", lambda E_, sq=sq, bnk=bnk, k=k: E_.matmul(
                    pk[bnk][0:1, 0:512], s["onesc2"][...], s[sq][...],
                    start=(k == 0), stop=(k == KD - 1)), "spe")
                self.release(sq, "spe", p)
            rms_half_done.append(p)

        rb_done = []
        for half in range((T + 511) // 512):
            hw = min(T, 512)
            tok = slice(half * hw, (half + 1) * hw)
            bnk = 6 + half
            rv = s["rows"][0:1, S:S + hw]
            rvb = s["rows_b"][0:1, 0:hw]
            self.need("dv", "row_bufs")
            self.wait("dv", "spe", rms_half_done[half])
            d = self.inc("dv", lambda E_, rv=rv, bnk=bnk, hw=hw: E_.tensor_scalar(
                rv, pk[bnk][0:1, 0:hw], 1.0 / D, RMS_EPS, ALU.mult, ALU.add), "sdve")
            self.wait("ac", "sdve", d)
            self.run("ac", lambda E_, rv=rv: E_.activation(out=rv, in_=rv, func=AF.Ln))
            a = self.inc("ac", lambda E_, rv=rv, rvb=rvb: E_.activation(
                out=rvb, in_=rv, func=AF.Exp, scale=-0.5), "sact")
            self.need("pe", f"bank{bnk}")
            self.wait("pe", "sact", a)
            p = self.inc("pe", lambda E_, rvb=rvb, bnk=bnk, hw=hw: E_.matmul(
                pk[bnk][:, 0:hw], s["onesrb"][...], rvb, start=True, stop=True), "spe")
            self.release("row_bufs", "spe", p)
            self.wait("dv", "spe", p)
            d = self.inc("dv", lambda E_, tok=tok, bnk=bnk, hw=hw: E_.tensor_copy(
                s["rb"][:, tok], pk[bnk][:, 0:hw]), "sdve")
            self.release(f"bank{bnk}", "sdve", d)
            rb_done.append(d)

        ya_pe_done = self.c["spe"]
        first = True
        for sub in range(nsub):
            tok = slice(sub * S, (sub + 1) * S)
            for k in range(KD):
                if first:
                    self.wait("dv", "spe", ya_pe_done)
                    first = False
                d = self.inc("dv", lambda E_, k=k, tok=tok: E_.scalar_tensor_tensor(
                    s["xn"][:, k, tok], s["zh"][:, k, tok], s["nw"][:, k:k + 1],
                    s["rb"][:, tok], ALU.mult, ALU.mult), "sdve")
            self.ln_done[sub] = d

        for sub in range(nsub):
            self.proj_pass("wy", sub, (0, 1, 2, 3, 4, 5))
        self.free_slot("ws0", self.c["spe"])
        tc = self.tap("d_x1", s["x0"][...], lambda dr, t=t: dr[:, :, t * T:(t + 1) * T], t)
        if tc:
            self.wait("dv", "sx", tc)

        self.ln_done = [0] * nsub
        self.layernorm(s["x0"], want_f32=True, extra_dv=self.wy_done,
                       xb_guard=self.c["spe"])
        self.ffn_ln_done = list(self.ln_done)
        tc = self.tap("d_xn2", s["xn"][...], lambda dr, t=t: dr[:, :, t * T:(t + 1) * T], t)
        if tc:
            self.wait("dv", "sx", tc)

        self.need("pe", "bank6")
        free_ps0 = free_ps1 = 0
        self.wait("pe", "sdve", self.ln_done[(T // S) - 1])
        for k in range(KD):
            for g in range(ngrp):
                gtok = slice(g * 128, (g + 1) * 128)
                fn = (lambda E_, k=k, g=g, gtok=gtok: E_.matmul(
                    pk[6][:, g * 8:(g + 1) * 8], s["zh"][:, k, gtok], s["rwt"][:, k, :],
                    start=(k == 0 and g == 0), stop=(k == KD - 1 and g == ngrp - 1)))
                if (k == 3 or k == KD - 1) and g == ngrp - 1:
                    p = self.inc("pe", fn, "spe")
                    if k == 3:
                        free_ps0 = p
                    else:
                        free_ps1 = p
                else:
                    self.run("pe", fn)
        self.free_slot("ps0", free_ps0)
        self.free_slot("ps1", free_ps1)
        self.wait("dv", "spe", free_ps1)
        d = self.inc("dv", lambda E_: E_.tensor_copy(s["gt"][...], pk[6][:, 0:ngrp * 8]),
                     "sdve")
        self.release("bank6", "sdve", d)
        self.run("dv", lambda E_: E_.tensor_copy(s["padd"][...], s["t8c"][...]))
        self.run("dv", lambda E_: E_.tensor_copy(s["padd"][...], s["t8c"][...]))
        for g in range(ngrp):
            gt = s["gt"][:, g * 8:(g + 1) * 8]
            for e in range(E):
                self.run("dv", lambda E_, gt=gt, e=e, g=g: E_.tensor_scalar(
                    s["junk8"][...], gt, gt[:, e:e + 1], 0.0, ALU.is_gt, ALU.add,
                    accum_out=s["cnts"][:, g * 8 + e:g * 8 + e + 1]))
        for g in range(ngrp):
            self.run("dv", lambda E_, g=g: E_.reduce_max(
                s["mxs"][:, g:g + 1], s["gt"][:, g * 8:(g + 1) * 8], AX.X))
        for g in range(ngrp):
            self.run("dv", lambda E_, g=g: E_.tensor_scalar(
                s["mskb"][:, g * 8:(g + 1) * 8], s["cnts"][:, g * 8:(g + 1) * 8],
                1.5, None, ALU.is_lt))
        for g in range(ngrp):
            self.run("dv", lambda E_, g=g: E_.tensor_scalar(
                s["ex"][:, g * 8:(g + 1) * 8], s["gt"][:, g * 8:(g + 1) * 8],
                s["mxs"][:, g:g + 1], None, ALU.subtract))
        d = self.pad_inc("dv", "sdve")
        self.wait("ac", "sdve", d)
        self.run("ac", lambda E_: E_.activation(
            out=s["ex"][...], in_=s["ex"][...], func=AF.Exp))
        a = self.pad_inc("ac", "sact")
        self.wait("dv", "sact", a)
        for g in range(ngrp):
            self.run("dv", lambda E_, g=g: E_.reduce_sum(
                s["sms"][:, g:g + 1], s["ex"][:, g * 8:(g + 1) * 8], AX.X))
        self.run("dv", lambda E_: E_.tensor_copy(s["padd"][...], s["t8c"][...]))
        self.run("dv", lambda E_: E_.tensor_copy(s["padd"][...], s["t8c"][...]))
        self.run("dv", lambda E_: E_.reciprocal(s["mxs"][...], s["sms"][...]))
        self.run("dv", lambda E_: E_.tensor_copy(s["padd"][...], s["t8c"][...]))
        self.run("dv", lambda E_: E_.tensor_copy(s["padd"][...], s["t8c"][...]))
        for g in range(ngrp):
            self.run("dv", lambda E_, g=g: E_.scalar_tensor_tensor(
                s["ex"][:, g * 8:(g + 1) * 8], s["ex"][:, g * 8:(g + 1) * 8],
                s["mxs"][:, g:g + 1], s["mskb"][:, g * 8:(g + 1) * 8],
                ALU.mult, ALU.mult))
        self.topk_done = d
        self.ew_ngrp = ngrp
        ng8 = (T // 128) * 8
        tc = self.tap("d_gt", s["gt"][...], lambda dr, t=t, ng8=ng8: dr[:, t * ng8:(t + 1) * ng8], t)
        tc2 = self.tap("d_ew", s["ew_fm"][...], lambda dr, t=t: dr[:, t * T:(t + 1) * T], t)
        if tc2:
            self.wait("dv", "sx", tc2)
        router_pe = self.c["spe"]
        self.free_slot("ps0", router_pe)
        self.free_slot("ps1", router_pe)

        for e in range(E):
            self.expert(t, e)

        xl = []
        for sub in range(nsub):
            self.wait("sy", "sdve", self.pass2_sub_done[sub])
            self.nstores += 1
            st = self.dma(
                "sy", self.dout[:, :, t * T + sub * S:t * T + (sub + 1) * S],
                s["x0"][:, :, sub * S:(sub + 1) * S], "sst")
            if t + 1 < self.ntiles:
                self.wait("sy", "sst", st)
                xl.append(("sx", self.dma(
                    "sy", s["x0"][:, :, sub * S:(sub + 1) * S],
                    self.dx[:, :, (t + 1) * T + sub * S:(t + 1) * T + (sub + 1) * S])))
        if t + 1 < self.ntiles:
            self.xload.append(xl)
            self.free_slot("ws0", self.c["spe"])
            self.free_slot("ws1", self.c["spe"])
            self.w_att["za"] = self.load_slot("ws0", self.dw["wza"])
            self.w_att["z"] = self.load_slot("ws1", self.dw["wz"])
            self.w_att["ya"] = self.load_slot("ps2", self.dw["wya"])


    def emit_ew_transpose(self):
        s, pk = self.s, self.pk
        ngrp = self.ew_ngrp
        d2 = 0
        for g in range(ngrp):
            if g % 4 == 0:
                self.need("pe", "bank7")
                self.wait("pe", "sdve", self.topk_done)
            p = self.inc("pe", lambda E_, g=g: E_.transpose(
                pk[7][0:8, (g % 4) * 128:(g % 4 + 1) * 128],
                s["ex"][:, g * 8:(g + 1) * 8], s["idn"][...]), "spe")
            if g % 4 == 3 or g == ngrp - 1:
                n0 = (g // 4) * 4
                self.wait("dv", "spe", p)
                d2 = self.inc("dv", lambda E_, n0=n0, g=g: E_.tensor_copy(
                    s["ew_fm"][:, n0 * 128:(g + 1) * 128],
                    pk[7][0:8, 0:(g + 1 - n0) * 128]), "sdve")
                self.release("bank7", "sdve", d2)
        self.ew_done = d2
    def ew_xe(self, e):
        s, pk = self.s, self.pk
        T = self.T
        if e == 0:
            self.emit_ew_transpose()
        self.need("gp", "ew_row")
        self.wait("gp", "sdve", self.ew_done)
        mv = self.dma("gp", s["ew_row"][...], s["ew_fm"][e:e + 1, :], "swg")
        for half in range((T + 511) // 512):
            hw = min(T, 512)
            tok = slice(half * hw, (half + 1) * hw)
            self.need("pe", "bank7")
            self.wait("pe", "swg", mv)
            p = self.inc("pe", lambda E_, tok=tok, hw=hw: E_.matmul(
                pk[7][:, 0:hw], s["onesrb"][...], s["ew_row"][0:1, tok],
                start=True, stop=True), "spe")
            self.wait("dv", "spe", p)
            self.need("dv", "ewb")
            d = self.inc("dv", lambda E_, tok=tok, hw=hw: E_.tensor_copy(
                s["ewb"][:, tok], pk[7][:, 0:hw]), "sdve")
            self.release("bank7", "sdve", d)
        self.release("ew_row", "spe", p)
        self.need("dv", "dgate")
        for k in range(KD):
            d = self.inc("dv", lambda E_, k=k: E_.tensor_mul(
                s["dgate"][:, k, :], s["xn"][:, k, :], s["ewb"][:, 0:T]), "sdve")
        self.release("ewb", "sdve", d)
        self.xe_done = d
        self.cur_xe = d

    def expert(self, t, e):
        s, pk = self.s, self.pk
        T, S = self.T, self.S
        nsub = T // S
        if e > 0:
            self.cur_xe = self.xe_done
        if not hasattr(self, "pass2_sub_done"):
            self.pass2_sub_done = [0] * nsub

        if e == 0:
            self.prefetch_gu((e, 1), e, 1, slot="ps0")
            self.prefetch_gu((e, 2), e, 2, slot="ps1")
        else:
            self.prefetch_gu((e, 1), e, 1)
            self.prefetch_gu((e, 2), e, 2)

        def emit_gate(j, l, sub):
            ic = j * 4 + l
            slot, v = self.gu_loaded[(e, j)]
            w = self.slot_t[slot]
            tok = slice(sub * S, (sub + 1) * S)
            bg = (ic % 3) * 2
            self.need("pe", f"bank{bg}")
            if l == 0 and sub == 0:
                self.wait("pe", "sdve", self.ffn_ln_done[sub])
                self.wait("pe", "swg", v)
            for k in range(KD):
                fn = (lambda E_, k=k, l=l, bg=bg, tok=tok, w=w: E_.matmul(
                    pk[bg][...], w[:, k, l * 128:(l + 1) * 128],
                    s["xn"][:, k, tok], start=(k == 0), stop=(k == KD - 1)))
                if k == KD - 1:
                    p = self.inc("pe", fn, "spe")
                else:
                    self.run("pe", fn)
            self.wait("ac", "spe", p)
            a = self.inc("ac", lambda E_, bg=bg, ic=ic, tok=tok: E_.activation(
                out=s["g"][:, ic, tok], in_=pk[bg][...], func=AF.Silu), "sact")
            self.release(f"bank{bg}", "sact", a)
            self.g_silu[(ic, sub)] = a

        def emit_up(j, l, sub):
            ic = j * 4 + l
            slot, v = self.gu_loaded[(e, j)]
            w = self.slot_t[slot]
            tok = slice(sub * S, (sub + 1) * S)
            bu = (ic % 3) * 2 + 1
            self.need("pe", f"bank{bu}")
            if l == 0 and sub == 0 and j in (0, 1) and e == 0:
                pass
            if (j, l, sub) == self.first_up:
                if e == 0:
                    self.ew_xe(0)
                self.wait("pe", "sdve", self.cur_xe)
            for k in range(KD):
                fn = (lambda E_, k=k, l=l, bu=bu, tok=tok, w=w: E_.matmul(
                    pk[bu][...], w[:, k, 512 + l * 128:512 + (l + 1) * 128],
                    s["dgate"][:, k, tok], start=(k == 0), stop=(k == KD - 1)))
                if k == KD - 1:
                    p = self.inc("pe", fn, "spe")
                else:
                    self.run("pe", fn)
            self.wait("dv", "spe", p)
            self.wait("dv", "sact", self.g_silu[(ic, sub)])
            d = self.inc("dv", lambda E_, ic=ic, bu=bu, tok=tok: E_.tensor_mul(
                s["g"][:, ic, tok], s["g"][:, ic, tok], pk[bu][...]), "sdve")
            self.release(f"bank{bu}", "sdve", d)
            self.last_g = d
            return p

        self.g_silu = {}
        self.last_g = 0
        slot_last = {}
        if e == 0:
            self.first_up = (0, 0, 0)
            order = ([("g", 0, l, 0) for l in range(4)] +
                     [("g", 1, l, 0) for l in range(4)] +
                     [("u", 0, l, 0) for l in range(4)] +
                     [("u", 1, l, 0) for l in range(4)] +
                     [(op, j, l, 1) for j in (0, 1) for l in range(4)
                      for op in ("g", "u")] +
                     [(op, j, l, sub) for j in (2, 3) for sub in range(nsub)
                      for l in range(4) for op in ("g", "u")])
            if nsub == 1:
                order = [o for o in order if o[3] == 0 or o[0] == "x"]
                order = ([("g", 0, l, 0) for l in range(4)] +
                         [("g", 1, l, 0) for l in range(4)] +
                         [("u", 0, l, 0) for l in range(4)] +
                         [("u", 1, l, 0) for l in range(4)] +
                         [(op, j, l, 0) for j in (2, 3)
                          for l in range(4) for op in ("g", "u")])
        else:
            self.first_up = (0, 0, 0)
            order = [(op, j, l, sub) for j in range(4) for sub in range(nsub)
                     for l in range(4) for op in ("g", "u")]
        w2q = 0
        for op, j, l, sub in order:
            if op == "g":
                emit_gate(j, l, sub)
            else:
                p = emit_up(j, l, sub)
                slot_last[j] = max(slot_last.get(j, 0), p)
                done_subs = sum(1 for (oo, jj, ll, ss) in order[:order.index((op, j, l, sub)) + 1]
                                if oo == "u" and jj == j)
                if done_subs == 4 * nsub:
                    self.free_slot(self.gu_loaded[(e, j)][0], slot_last[j])
                    if j == 0:
                        self.prefetch_gu((e, 3), e, 3)
                    elif j == 1:
                        self.prefetch_w2(e, 0)
                    elif j == 2:
                        self.prefetch_w2(e, 1)
        g_done = self.last_g
        pass1_pe = self.c["spe"]
        self.release("dgate", "spe", pass1_pe)
        self.xn_free_pe = pass1_pe
        if e + 1 < E:
            self.ew_xe(e + 1)

        sl0, v0 = self.w2_loaded[(e, 0)]
        sl1, v1 = self.w2_loaded[(e, 1)]
        w2a, w2b = self.slot_t[sl0], self.slot_t[sl1]
        last_p = 0
        for sub in range(nsub):
            tok = slice(sub * S, (sub + 1) * S)
            for m in range(KD):
                b = 4 + (m % 2)
                self.need("pe", f"bank{b}")
                if sub == 0 and m == 0:
                    self.wait("pe", "swg", max(v0, v1))
                    self.wait("pe", "sdve", g_done)
                for ic in range(ICN):
                    w2s = w2a if ic < 8 else w2b
                    icl = ic % 8
                    fn = (lambda E_, icl=icl, m=m, b=b, tok=tok, w2s=w2s, ic=ic: E_.matmul(
                        pk[b][...], w2s[:, icl, m * 128:(m + 1) * 128],
                        s["g"][:, ic, tok], start=(ic == 0), stop=(ic == ICN - 1)))
                    if ic == ICN - 1:
                        p = self.inc("pe", fn, "spe")
                    else:
                        self.run("pe", fn)
                last_p = p
                self.wait("dv", "spe", p)
                d = self.inc("dv", lambda E_, m=m, b=b, tok=tok:
                             E_.tensor_add(s["x0"][:, m, tok], s["x0"][:, m, tok],
                                           pk[b][...]), "sdve")
                self.release(f"bank{b}", "sdve", d)
            self.pass2_sub_done[sub] = d
        self.free_slot(sl0, last_p)
        self.free_slot(sl1, last_p)
        if e + 1 < E:
            self.prefetch_gu((e + 1, 0), e + 1, 0)


def _tables(C):
    d = np.arange(D, dtype=np.float64)
    th = 0.001 ** (d / (D - 1))
    j = np.arange(C, dtype=np.float64)[None, :]
    thj = th[:, None] * j
    cr = np.cos(thj)
    sr = np.sin(thj)
    f0 = np.exp(1j * th)
    rc = np.exp(1j * th * C)
    return cr, sr, f0.real, f0.imag, rc.real, rc.imag


def _pm2(v):
    return np.ascontiguousarray(np.asarray(v, np.float64).reshape(KD, 128).T,
                                dtype=np.float32)


def _pm3(m):
    return np.ascontiguousarray(
        np.asarray(m).reshape(KD, 128, m.shape[1]).transpose(1, 0, 2),
        dtype=np.float32)


def make_inputs(L, C, x_b, Wz, bz, Wza, bza, Wy, by, Wya, bya, nw, lcr, lci,
                w1, w2, rw):
    cr, sr, f0r, f0i, rcr, rci = _tables(C)
    com = {
        "wz": _pm3(np.ascontiguousarray(Wz.T)),
        "wza": _pm3(np.ascontiguousarray(Wza.T)),
        "wy": _pm3(np.ascontiguousarray(Wy.T)),
        "wya": _pm3(np.ascontiguousarray(Wya.T)),
        "bz": _pm2(bz), "bza": _pm2(bza), "by": _pm2(by), "bya": _pm2(bya),
        "nw": _pm2(nw), "lcr": _pm2(lcr), "lci": _pm2(lci),
        "f0r": _pm2(f0r), "f0i": _pm2(f0i),
        "rcr": _pm2(rcr), "rci": _pm2(rci),
        "cr": _pm3(cr), "sr": _pm3(sr),
        "rwt": _pm3(np.ascontiguousarray(rw.T)),
        "idn": np.eye(128, dtype=np.float32),
        "onesr": np.ones((1, 128), np.float32),
        "onesrb": np.ones((1, 128), ml_dtypes.bfloat16),
        "onesc": np.ones((128, 1), np.float32),
        "onesc2": np.ones((128, 1), ml_dtypes.bfloat16),
    }
    w1t = np.empty((E, 128, KD, I2), np.float32)
    for e_ in range(E):
        wt = np.ascontiguousarray(w1[e_].T)
        re = np.empty((D, I2), np.float32)
        for jj in range(4):
            re[:, jj * 1024:jj * 1024 + 512] = wt[:, jj * 512:(jj + 1) * 512]
            re[:, jj * 1024 + 512:(jj + 1) * 1024] = \
                wt[:, 2048 + jj * 512:2048 + (jj + 1) * 512]
        w1t[e_] = _pm3(re)
    w2t = np.empty((E, 128, ICN, D), np.float32)
    for e_ in range(E):
        w2t[e_] = np.ascontiguousarray(
            w2[e_].T.reshape(ICN, 128, D).transpose(1, 0, 2))
    com["w1"] = w1t
    com["w2"] = w2t
    xs = [np.ascontiguousarray(
        x_b[b].T.reshape(KD, 128, L).transpose(1, 0, 2), np.float32)
        for b in range(x_b.shape[0])]
    return com, xs


def run(x_full, Wz, bz, Wza, bza, Wy, by, Wya, bya, nw, lcr, lci, w1, w2, rw,
        T=1024, C=256, debug=False):
    B, L, _ = x_full.shape
    com, xs = make_inputs(L, C, x_full, Wz, bz, Wza, bza, Wy, by, Wya, bya,
                          nw, lcr, lci, w1, w2, rw)
    bld = Builder(L, T, C, debug=debug)
    nc = bld.build()
    in_maps = [dict(com, x0=xs[b]) for b in range(B)]
    res = run_bass_kernel_spmd(nc, in_maps, core_ids=list(range(B)))
    outs = []
    for b in range(B):
        o = np.asarray(res.results[b]["out"])
        outs.append(np.ascontiguousarray(
            o.transpose(1, 0, 2).reshape(D, L).T))
    if debug:
        return np.stack(outs), res.results
    return np.stack(outs)


def kernel(L0_attn_x, L0_attn_Wz, L0_attn_bz, L0_attn_Wza, L0_attn_bza,
           L0_attn_Wy, L0_attn_by, L0_attn_Wya, L0_attn_bya, L0_attn_norm_w,
           L0_attn_lc_re, L0_attn_lc_im, L0_ffn_w1, L0_ffn_w2, L0_ffn_router_w):
    x = np.asarray(L0_attn_x, np.float32)
    out = run(x,
              np.asarray(L0_attn_Wz, np.float32), np.asarray(L0_attn_bz, np.float32),
              np.asarray(L0_attn_Wza, np.float32), np.asarray(L0_attn_bza, np.float32),
              np.asarray(L0_attn_Wy, np.float32), np.asarray(L0_attn_by, np.float32),
              np.asarray(L0_attn_Wya, np.float32), np.asarray(L0_attn_bya, np.float32),
              np.asarray(L0_attn_norm_w, np.float32),
              np.asarray(L0_attn_lc_re, np.float32), np.asarray(L0_attn_lc_im, np.float32),
              np.asarray(L0_ffn_w1, np.float32), np.asarray(L0_ffn_w2, np.float32),
              np.asarray(L0_ffn_router_w, np.float32))
    return out.astype(np.float32)


# version 13
# speedup vs baseline: 1.0824x; 1.0824x over previous
import numpy as np
import ml_dtypes
from contextlib import ExitStack

import concourse.bass as bass
import concourse.mybir as mybir
from concourse.bass_utils import run_bass_kernel_spmd

F32 = mybir.dt.float32
BF16 = mybir.dt.bfloat16
AF = mybir.ActivationFunctionType
ALU = mybir.AluOpType
AX = mybir.AxisListType

D = 1024
KD = 8
E = 8
I2 = 4096
ICN = 16
LN_EPS = 1e-5
RMS_EPS = 1e-6
PH_EPS = 1e-5


def _align(x, a=64):
    return (x + a - 1) // a * a


class Builder:
    ENGS = ("sy", "gp", "pe", "ac", "dv")
    SEMS = ("sx", "swg", "spe", "sact", "sdve", "sst", "sgc")

    def __init__(self, L, T, C, S=512, debug=False):
        assert T % S == 0 and T % C == 0 and L % T == 0 and T <= 1024
        self.L, self.T, self.C, self.S = L, T, C, S
        self.ntiles = L // T
        self.q = {e: [] for e in self.ENGS}
        self.c = {s: 0 for s in self.SEMS}
        self.rel = {}
        self.sem = {}
        self.debug = debug
        self.paranoid = False
        self.dbg_out = {}

    def tap(self, name, sbuf_ap, dram_slice_fn, t):
        if not self.debug:
            return 0
        dr = self.dbg_out[name]
        self.wait("sy", "sdve", self.c["sdve"])
        self.wait("sy", "sact", self.c["sact"])
        return self.dma("sy", dram_slice_fn(dr, t), sbuf_ap)

    def run(self, eng, fn):
        self.q[eng].append(fn)

    def wait(self, eng, sem_name, val):
        assert val <= self.c[sem_name], (eng, sem_name, val, self.c[sem_name])
        if val <= 0:
            return
        self.q[eng].append(lambda EE, s=sem_name, v=val: EE.wait_ge(self.sem[s], v))

    def inc(self, eng, fn, sem_name="spe" , n=1):
        self.c[sem_name] += n
        v = self.c[sem_name]
        self.q[eng].append(
            lambda EE, f=fn, s=sem_name, k=n: f(EE).then_inc(self.sem[s], k))
        return v

    def dma(self, eng, out_ap, in_ap, sem="sx"):
        return self.inc(eng, lambda EE, o=out_ap, i=in_ap: EE.dma_start(out=o, in_=i),
                        sem, 16)

    def pad_inc(self, eng, sem_name, n=2):
        s = self.s
        for _ in range(n):
            if eng == "dv":
                self.run("dv", lambda E_: E_.tensor_copy(s["padd"][...], s["t8c"][...]))
            else:
                self.run("ac", lambda E_: E_.activation(out=s["pada"][...],
                                                        in_=s["padd"][...], func=AF.Copy))
        if eng == "dv":
            return self.inc("dv", lambda E_: E_.tensor_copy(s["padd"][...], s["t8c"][...]),
                            sem_name)
        return self.inc("ac", lambda E_: E_.activation(out=s["pada"][...],
                                                       in_=s["padd"][...], func=AF.Copy),
                        sem_name)

    def need(self, eng, res):
        r = self.rel.get(res)
        if r:
            self.wait(eng, r[0], r[1])

    def release(self, res, sem_name, val):
        self.rel[res] = (sem_name, val)

    def build(self):
        L, T, C, S = self.L, self.T, self.C, self.S
        nc = bass.Bass("TRN2", target_bir_lowering=False)
        self.nc = nc

        dx = nc.dram_tensor("x0", [128, L // S, KD, S], F32, kind="ExternalInput").ap()
        dout = nc.dram_tensor("out", [128, L // S, KD, S], F32,
                              kind="ExternalOutput").ap()
        dw = {}
        for n in ("wz", "wza", "wy", "wya"):
            dw[n] = nc.dram_tensor(n, [128, KD * D], F32, kind="ExternalInput").ap()
        for n in ("bz", "bza", "by", "bya", "nw", "lcr", "lci",
                  "f0r", "f0i", "rcr", "rci"):
            dw[n] = nc.dram_tensor(n, [128, KD], F32, kind="ExternalInput").ap()
        for n in ("cr", "sr"):
            dw[n] = nc.dram_tensor(n, [128, KD, C], F32, kind="ExternalInput").ap()
        dw["w1"] = nc.dram_tensor("w1", [E, 4, 128, KD * 1024], F32,
                                  kind="ExternalInput").ap()
        dw["w2"] = nc.dram_tensor("w2", [E, 2, 128, 8 * 1024], F32,
                                  kind="ExternalInput").ap()
        dw["rwt"] = nc.dram_tensor("rwt", [128, KD, E], F32, kind="ExternalInput").ap()
        dw["idn"] = nc.dram_tensor("idn", [128, 128], F32, kind="ExternalInput").ap()
        dw["onesr"] = nc.dram_tensor("onesr", [1, 128], F32, kind="ExternalInput").ap()
        dw["onesrb"] = nc.dram_tensor("onesrb", [1, 128], BF16, kind="ExternalInput").ap()
        dw["onesc"] = nc.dram_tensor("onesc", [128, 1], F32, kind="ExternalInput").ap()
        dw["onesc2"] = nc.dram_tensor("onesc2", [128, 1], BF16, kind="ExternalInput").ap()
        self.dx, self.dout, self.dw = dx, dout, dw
        if self.debug:
            for n, shp, dt in (("d_xn", [128, KD, L], BF16), ("d_z", [128, KD, L], F32),
                               ("d_h", [128, KD, L], F32), ("d_x1", [128, KD, L], F32),
                               ("d_gt", [128, (L // 128) * 8], F32), ("d_ew", [8, L], BF16),
                               ("d_xn2", [128, KD, L], BF16), ("d_g", [128, ICN, L], BF16)):
                self.dbg_out[n] = nc.dram_tensor(n, shp, dt, kind="ExternalOutput").ap()
            self.dbg_out["d_ini"] = nc.dram_tensor("d_ini", [128, 64], F32,
                                                   kind="ExternalOutput").ap()

        off = [16512]

        def sb(name, shape, dtype, at=None):
            sz = int(np.prod(shape[1:])) * (4 if dtype == F32 else 2)
            if at is None:
                at = off[0]
                off[0] = _align(at + sz)
            return nc.alloc_sbuf_tensor_at(name, list(shape), dtype, offset=at)

        s = {}
        s["x0"] = sb("x0s", [128, T // S, KD, S], F32)
        xn_base = off[0]
        s["xn"] = sb("xns", [128, KD, T], BF16)
        zh_base = off[0]
        s["zh"] = sb("zhs", [128, KD, T], F32)
        ps_bytes = 2 * KD * D
        self.ps = [sb(f"ps{i}", [128, KD, D], BF16, at=zh_base + i * ps_bytes)
                   for i in range(3)]
        self.ps_flat = [nc.alloc_sbuf_tensor_at(f"psf{i}", [128, KD * D], BF16,
                                                offset=zh_base + i * ps_bytes)
                        for i in range(3)]
        off[0] = max(off[0], _align(zh_base + 3 * ps_bytes))
        dg_base = off[0]
        s["dgate"] = sb("dgate", [128, KD, T], BF16)
        e_base = off[0]
        s["g"] = sb("gts", [128, ICN, T], BF16)
        self.ws = [sb(f"ws{i}", [128, KD, D], BF16, at=e_base + i * ps_bytes)
                   for i in range(2)]
        self.ws_flat = [nc.alloc_sbuf_tensor_at(f"wsf{i}", [128, KD * D], BF16,
                                                offset=e_base + i * ps_bytes)
                        for i in range(2)]
        off[0] = max(off[0], _align(e_base + max(2 * ps_bytes, 2 * ICN * T)))
        for n in ("cr", "sr"):
            s[n] = sb(n + "s", [128, KD, C], F32)
        s["szb"] = sb("szb", [128, S], F32)
        s["szb2"] = sb("szb2", [128, S], F32)
        s["szbb"] = sb("szbb", [128, S], BF16)
        s["szb2b"] = sb("szb2b", [128, S], BF16)
        for n in ("u_r", "u_i", "c_r", "c_i", "m_r", "m_i"):
            s[n] = sb(n, [128, 2, C], F32)
        s["rb"] = sb("rbs", [128, T], F32)
        s["ab"] = sb("abs", [128, T], F32)
        s["ewb"] = sb("ewbs", [128, T], BF16)
        s["rows"] = sb("rows", [1, 3 * S], F32)
        s["rows_b"] = sb("rows_b", [1, 2 * S], BF16)
        s["ew_fm"] = sb("ew_fm", [8, T], BF16)
        s["ew_row"] = sb("ew_row", [1, T], BF16)
        ngrp = T // 128
        s["gt"] = sb("gt", [128, ngrp * 8], F32)
        s["ex"] = sb("ex", [128, ngrp * 8], F32)
        s["mskb"] = sb("mskb", [128, ngrp * 8], F32)
        s["cnts"] = sb("cnts", [128, ngrp * 8], F32)
        s["junk8"] = sb("junk8", [128, 8], F32)
        s["cnt8"] = sb("cnt8", [128, 8], F32)
        s["mx"] = sb("mx", [128, 1], F32)
        s["sm"] = sb("sm", [128, 1], F32)
        s["rc"] = sb("rc", [128, 1], F32)
        s["ewsb"] = sb("ewsb", [128, 8], F32)
        s["q_r"] = sb("q_r", [128, 2, KD], F32)
        s["q_i"] = sb("q_i", [128, 2, KD], F32)
        for n in ("bz", "bza", "by", "bya", "nw", "lcr", "lci",
                  "f0r", "f0i", "rcr", "rci"):
            s[n] = sb(n + "s", [128, KD], F32)
        s["rwt"] = sb("rwts", [128, KD, E], F32)
        s["idn"] = sb("idns", [128, 128], F32)
        s["onesr"] = sb("onesrs", [1, 128], F32)
        s["onesrb"] = sb("onesrbs", [1, 128], BF16)
        s["onesc"] = sb("onescs", [128, 1], F32)
        s["onesc2"] = sb("onesc2s", [128, 1], BF16)
        s["ones_sc"] = sb("ones_sc", [128, C], F32)
        s["t8a"] = sb("t8a", [128, 8], F32)
        s["t8b"] = sb("t8b", [128, 8], F32)
        s["t8c"] = sb("t8c", [128, 8], F32)
        s["t8d"] = sb("t8d", [128, 8], F32)
        s["padd"] = sb("padd", [128, 8], F32)
        s["pada"] = sb("pada", [128, 8], F32)
        s["mxs"] = sb("mxs", [128, T // 128], F32)
        s["sms"] = sb("sms", [128, T // 128], F32)
        assert off[0] <= 229344, off[0]
        self.s = s
        self.pk = [nc.alloc_psum_tensor(f"pk{i}", [128, 512], F32) for i in range(8)]
        self.slot_t = {"ps0": self.ps[0], "ps1": self.ps[1], "ps2": self.ps[2],
                       "ws0": self.ws[0], "ws1": self.ws[1]}
        self.slot_f = {"ps0": self.ps_flat[0], "ps1": self.ps_flat[1],
                       "ps2": self.ps_flat[2], "ws0": self.ws_flat[0],
                       "ws1": self.ws_flat[1]}
        self.slot_busy = set()

        with ExitStack() as ctx:
            for sm in self.SEMS:
                self.sem[sm] = ctx.enter_context(nc.semaphore(sm))
            self.schedule()
            with nc.Block() as block:
                @block.sync
                def _(eng):
                    for fn in self.q["sy"]:
                        fn(eng)

                @block.gpsimd
                def _(eng):
                    for fn in self.q["gp"]:
                        fn(eng)

                @block.tensor
                def _(eng):
                    for fn in self.q["pe"]:
                        fn(eng)

                @block.scalar
                def _(eng):
                    for fn in self.q["ac"]:
                        fn(eng)

                @block.vector
                def _(eng):
                    for fn in self.q["dv"]:
                        fn(eng)
        return nc

    def schedule(self):
        s = self.s
        T = self.T

        for n in ("onesc2", "onesrb", "onesc", "onesr"):
            self.dma("sy", s[n][...], self.dw[n])
        nsub0 = T // self.S
        self.xload = [[("sx", self.dma("sy", s["x0"][:, u], self.dx[:, u]))
                       for u in range(nsub0)]]
        for n in ("bz", "bza", "by", "bya", "nw", "lcr", "lci",
                  "f0r", "f0i", "rcr", "rci", "rwt", "idn", "cr", "sr"):
            self.dma("sy", s[n][...], self.dw[n])
        self.init_sx = self.c["sx"]

        self.w_att = {}
        self.w_att["za"] = self.load_slot("ws0", self.dw["wza"])
        self.w_att["z"] = self.load_slot("ws1", self.dw["wz"])
        self.w_att["ya"] = self.load_slot("ps2", self.dw["wya"])

        self.run("dv", lambda E_: E_.memset(s["ones_sc"][...], float(np.exp(PH_EPS))))
        self.wait("dv", "sx", self.init_sx)
        self.run("dv", lambda E_: E_.tensor_mul(s["t8a"][...], s["lcr"][...], s["f0r"][...]))
        self.run("dv", lambda E_: E_.tensor_mul(s["t8b"][...], s["lci"][...], s["f0i"][...]))
        self.run("dv", lambda E_: E_.tensor_mul(s["t8c"][...], s["lcr"][...], s["f0i"][...]))
        self.run("dv", lambda E_: E_.tensor_mul(s["t8d"][...], s["lci"][...], s["f0r"][...]))
        self.run("dv", lambda E_: E_.tensor_sub(s["q_r"][:, 0, :], s["t8a"][...], s["t8b"][...]))
        self.run("dv", lambda E_: E_.tensor_add(s["q_i"][:, 0, :], s["t8c"][...], s["t8d"][...]))
        self.run("dv", lambda E_: E_.tensor_copy(s["padd"][...], s["t8c"][...]))
        self.run("dv", lambda E_: E_.tensor_copy(s["padd"][...], s["t8c"][...]))
        self.run("dv", lambda E_: E_.tensor_scalar_mul(s["q_i"][:, 0, :], s["q_i"][:, 0, :], -1.0))
        if self.debug:
            d = self.inc("dv", lambda E_: E_.tensor_copy(s["junk8"][...], s["q_i"][:, 0, :]),
                         "sdve")
            self.wait("sy", "sdve", d)
            dr = self.dbg_out["d_ini"]
            for i, nm in enumerate(("lcr", "lci", "f0r", "f0i", "rcr", "rci")):
                self.dma("sy", dr[:, i * 8:(i + 1) * 8], s[nm][...])
            tc = self.dma("sy", dr[:, 48:56], s["q_r"][:, 0, :])
            tc = self.dma("sy", dr[:, 56:64], s["q_i"][:, 0, :])
            self.wait("dv", "sx", tc)

        self.nstores = 0
        self.gu_loaded = {}
        self.w2_loaded = {}
        for t in range(self.ntiles):
            self.tile(t)
        self.wait("sy", "sst", 16 * self.nstores)

    def layernorm(self, src, want_f32, extra_dv=0, xb_guard=0):
        s, pk = self.s, self.pk
        self.xb_guard = xb_guard
        T, S = self.T, self.S
        for sub in range(T // S):
            tok = slice(sub * S, (sub + 1) * S)
            for k in range(KD):
                if k == 0:
                    self.wait("dv", *self.xsrc_ready[sub])
                    if sub == 0 and extra_dv:
                        self.wait("dv", "sdve", extra_dv)
                    if self.xb_guard:
                        self.wait("dv", "spe", self.xb_guard)
                fn = (lambda E_, k=k, tok=tok, sub=sub: E_.tensor_copy(
                    s["xn"][:, k, tok], src[:, sub, k, :]))
                if k == KD - 1:
                    xb_d = self.inc("dv", fn, "sdve")
                else:
                    self.run("dv", fn)
            for k in range(KD):
                sq = ("szbb", "szb2b")[k % 2]
                self.need("ac", sq)
                if k == 0:
                    self.wait("ac", "sdve", xb_d)
                a = self.inc("ac", lambda E_, k=k, sq=sq, tok=tok: E_.activation(
                    out=s[sq][...], in_=s["xn"][:, k, tok],
                    func=AF.Square), "sact")
                if k == 0:
                    self.need("pe", "bank6")
                    self.need("pe", "bank7")
                    self.wait("pe", "sdve", xb_d)
                self.wait("pe", "sact", a)
                p = self.inc("pe", lambda E_, k=k, sq=sq: E_.matmul(
                    pk[6][0:1, 0:S], s["onesc2"][...], s[sq][...],
                    start=(k == 0), stop=(k == KD - 1)), "spe")
                self.release(sq, "spe", p)
                p = self.inc("pe", lambda E_, k=k, tok=tok: E_.matmul(
                    pk[7][0:1, 0:S], s["onesc2"][...], s["xn"][:, k, tok],
                    start=(k == 0), stop=(k == KD - 1)), "spe")
            stats_done = p
            rm = s["rows"][0:1, 0:S]
            rv = s["rows"][0:1, S:2 * S]
            rcx = s["rows"][0:1, 2 * S:3 * S]
            rvb = s["rows_b"][0:1, 0:S]
            rab = s["rows_b"][0:1, S:2 * S]
            self.need("dv", "row_bufs")
            self.wait("dv", "spe", stats_done)
            self.run("dv", lambda E_, rm=rm: E_.tensor_scalar_mul(rm, pk[7][0:1, 0:S], 1.0 / D))
            self.run("dv", lambda E_, rv=rv: E_.tensor_scalar(
                rv, pk[6][0:1, 0:S], 1.0 / D, LN_EPS, ALU.mult, ALU.add))
            self.run("dv", lambda E_, rm=rm, rcx=rcx: E_.tensor_mul(rcx, rm, rm))
            d = self.inc("dv", lambda E_, rv=rv, rcx=rcx: E_.tensor_sub(rv, rv, rcx), "sdve")
            self.release("bank6", "sdve", d)
            self.release("bank7", "sdve", d)
            self.wait("ac", "sdve", d)
            self.run("ac", lambda E_, rv=rv: E_.activation(out=rv, in_=rv, func=AF.Ln))
            a = self.inc("ac", lambda E_, rv=rv, rvb=rvb: E_.activation(
                out=rvb, in_=rv, func=AF.Exp, scale=-0.5), "sact")
            self.wait("dv", "sact", a)
            d = self.inc("dv", lambda E_, rm=rm, rvb=rvb, rab=rab: E_.tensor_mul(
                rab, rm, rvb), "sdve")
            self.need("pe", "bank6")
            self.need("pe", "bank7")
            self.wait("pe", "sdve", d)
            self.inc("pe", lambda E_, rvb=rvb: E_.matmul(
                pk[6][:, 0:S], s["onesrb"][...], rvb, start=True, stop=True), "spe")
            p = self.inc("pe", lambda E_, rab=rab: E_.matmul(
                pk[7][:, 0:S], s["onesrb"][...], rab, start=True, stop=True), "spe")
            self.release("row_bufs", "spe", p)
            self.wait("dv", "spe", p)
            self.run("dv", lambda E_, tok=tok: E_.tensor_copy(s["rb"][:, tok], pk[6][:, 0:S]))
            d = self.inc("dv", lambda E_, tok=tok: E_.tensor_copy(s["ab"][:, tok], pk[7][:, 0:S]),
                         "sdve")
            self.release("bank6", "sdve", d)
            self.release("bank7", "sdve", d)
            for k in range(KD):
                self.run("dv", lambda E_, k=k, tok=tok, sub=sub: E_.tensor_mul(
                    s["szb"][...], src[:, sub, k, :], s["rb"][:, tok]))
                if want_f32:
                    self.run("dv", lambda E_, k=k, tok=tok: E_.tensor_sub(
                        s["zh"][:, k, tok], s["szb"][...], s["ab"][:, tok]))
                    fn = (lambda E_, k=k, tok=tok: E_.tensor_copy(
                        s["xn"][:, k, tok], s["zh"][:, k, tok]))
                else:
                    fn = (lambda E_, k=k, tok=tok: E_.tensor_sub(
                        s["xn"][:, k, tok], s["szb"][...], s["ab"][:, tok]))
                if k == KD - 1:
                    self.ln_done[sub] = self.inc("dv", fn, "sdve")
                else:
                    self.run("dv", fn)
            self.release("szb", "sdve", self.ln_done[sub])

    def proj_pass(self, kind, sub, banks):
        s, pk = self.s, self.pk
        S = self.S
        tok = slice(sub * S, (sub + 1) * S)
        wslot, wcnt = self.w_att[kind]
        w = self.slot_t[wslot]
        for m in range(KD):
            b = banks[m % len(banks)]
            self.need("pe", f"bank{b}")
            if m == 0 and sub == 0:
                self.wait("pe", "swg", wcnt)
            if m == 0:
                self.wait("pe", "sdve", self.ln_done[sub])
            for k in range(KD):
                fn = (lambda E_, k=k, m=m, b=b, tok=tok: E_.matmul(
                    pk[b][...], w[:, k, m * 128:(m + 1) * 128], s["xn"][:, k, tok],
                    start=(k == 0), stop=(k == KD - 1)))
                if k == KD - 1:
                    p = self.inc("pe", fn, "spe")
                else:
                    self.run("pe", fn)
            if kind == "za":
                self.wait("ac", "spe", p)
                a = self.inc("ac", lambda E_, m=m, b=b, tok=tok: E_.activation(
                    out=s["zh"][:, m, tok], in_=pk[b][...], func=AF.Silu,
                    bias=s["bza"][:, m:m + 1]), "sact")
                self.release(f"bank{b}", "sact", a)
                self.za_done = a
            elif kind == "z":
                self.wait("dv", "spe", p)
                self.wait("dv", "sact", self.za_done)
                d = self.inc("dv", lambda E_, m=m, b=b, tok=tok: E_.scalar_tensor_tensor(
                    s["zh"][:, m, tok], pk[b][...], s["bz"][:, m:m + 1],
                    s["zh"][:, m, tok], ALU.add, ALU.mult), "sdve")
                self.release(f"bank{b}", "sdve", d)
                self.z_done_sub[sub] = d
            elif kind == "ya":
                self.wait("ac", "spe", p)
                a = self.inc("ac", lambda E_, m=m, b=b, tok=tok: E_.activation(
                    out=s["dgate"][:, m, tok], in_=pk[b][...], func=AF.Silu,
                    bias=s["bya"][:, m:m + 1]), "sact")
                self.release(f"bank{b}", "sact", a)
                self.ya_done = a
            elif kind == "wy":
                self.wait("dv", "spe", p)
                if m == 0 and sub == 0:
                    self.wait("dv", "sact", self.ya_done)
                self.run("dv", lambda E_, m=m, b=b, tok=tok: E_.scalar_tensor_tensor(
                    s["szb"][...], pk[b][...], s["by"][:, m:m + 1],
                    s["dgate"][:, m, tok], ALU.add, ALU.mult))
                d = self.inc("dv", lambda E_, m=m, sub=sub: E_.tensor_add(
                    s["x0"][:, sub, m, :], s["x0"][:, sub, m, :], s["szb"][...]),
                    "sdve")
                self.release(f"bank{b}", "sdve", d)
                self.release("szb", "sdve", d)
                self.wy_done = d

    def load_slot(self, name, dram_ap):
        self.need("gp", name)
        v = self.dma("gp", self.slot_f[name][...], dram_ap, "swg")
        self.slot_busy.add(name)
        return (name, v)

    def free_slot(self, name, pe_val):
        self.release(name, "spe", pe_val)
        self.slot_busy.discard(name)

    def pick_ps(self):
        best, bc = None, None
        for n in ("ps0", "ps1", "ps2"):
            if n in self.slot_busy:
                continue
            c = self.rel.get(n, ("spe", 0))[1]
            if bc is None or c < bc:
                best, bc = n, c
        assert best is not None, "no free ps slot"
        return best

    def prefetch_gu(self, key, e, j, slot=None):
        if slot is None:
            slot = self.pick_ps()
        self.gu_loaded[key] = self.load_slot(slot, self.dw["w1"][e, j])

    def prefetch_w2(self, e, half, slot=None):
        if slot is None:
            slot = self.pick_ps()
        self.w2_loaded[(e, half)] = self.load_slot(slot, self.dw["w2"][e, half])

    def tile(self, t):
        s, pk = self.s, self.pk
        T, C, S = self.T, self.C, self.S
        nsub = T // S
        ncc = T // C
        ngrp = T // 128
        self.ln_done = [0] * nsub
        self.z_done_sub = [0] * nsub

        self.xsrc_ready = self.xload[t]
        self.layernorm(s["x0"], want_f32=False,
                       xb_guard=getattr(self, "xn_free_pe", 0) or self.c["spe"])
        tc = self.tap("d_xn", s["xn"][...], lambda dr, t=t: dr[:, :, t * T:(t + 1) * T], t)
        if tc:
            self.wait("dv", "sx", tc)

        for sub in range(nsub):
            self.proj_pass("za", sub, (0, 2, 4))
            self.proj_pass("z", sub, (1, 3, 5))
        tc = self.tap("d_z", s["zh"][...], lambda dr, t=t: dr[:, :, t * T:(t + 1) * T], t)
        if tc:
            self.wait("dv", "sx", tc)
        zaz_pe = self.c["spe"]
        self.free_slot("ws0", zaz_pe)
        self.free_slot("ws1", zaz_pe)

        for sub in range(nsub):
            self.proj_pass("ya", sub, (0, 1, 2, 3, 4, 5))
        self.free_slot("ps2", self.c["spe"])

        groups = [(cc, k) for cc in range(ncc) for k in range(KD)]
        last_zwait = 0
        NG = len(groups)
        h_cnt = {}
        scan_ms = {}
        m_ms = {}
        u_ms = {}
        for gi in range(NG + 1):
            if gi < NG:
                cc, k = groups[gi]
                p2 = gi % 2
                ck = slice(cc * C, (cc + 1) * C)
                zneed = self.z_done_sub[(cc * C) // self.S]
                if zneed > last_zwait:
                    self.wait("gp", "sdve", zneed)
                    last_zwait = zneed
                if gi >= 2:
                    self.wait("gp", "sdve", scan_ms[gi - 2])
                self.run("gp", lambda E_, k=k, ck=ck, p2=p2: E_.tensor_mul(
                    s["u_r"][:, p2, :], s["zh"][:, k, ck], s["cr"][:, k, :]))
                u_ms[gi] = self.inc("gp", lambda E_, k=k, ck=ck, p2=p2: E_.tensor_mul(
                    s["u_i"][:, p2, :], s["zh"][:, k, ck], s["sr"][:, k, :]), "sgc")
            if gi >= 1:
                pkk = groups[gi - 1][1]
                pp = (gi - 1) % 2
                self.wait("gp", "sdve", scan_ms[gi - 1])
                if gi >= 3:
                    self.wait("gp", "sdve", h_cnt[groups[gi - 3]])
                self.run("gp", lambda E_, pkk=pkk, pp=pp: E_.tensor_mul(
                    s["m_r"][:, pp, :], s["cr"][:, pkk, :], s["c_r"][:, pp, :]))
                m_ms[gi - 1] = self.inc("gp", lambda E_, pkk=pkk, pp=pp: E_.tensor_mul(
                    s["m_i"][:, pp, :], s["sr"][:, pkk, :], s["c_i"][:, pp, :]), "sgc")
            if gi < NG:
                cc, k = groups[gi]
                p2 = gi % 2
                par = (t * ncc + cc) % 2
                self.wait("dv", "sgc", u_ms[gi])
                if gi >= 2:
                    self.wait("dv", "sgc", m_ms[gi - 2])
                self.run("dv", lambda E_, k=k, p2=p2, par=par: E_.tensor_tensor_scan(
                    s["c_r"][:, p2, :], s["ones_sc"][...], s["u_r"][:, p2, :],
                    s["q_r"][:, par, k:k + 1], ALU.mult, ALU.add))
                scan_ms[gi] = self.inc("dv", lambda E_, k=k, p2=p2, par=par:
                                       E_.tensor_tensor_scan(
                    s["c_i"][:, p2, :], s["ones_sc"][...], s["u_i"][:, p2, :],
                    s["q_i"][:, par, k:k + 1], ALU.mult, ALU.add), "sdve")
                self.run("dv", lambda E_, k=k, p2=p2: E_.tensor_scalar(
                    s["t8a"][:, 0:1], s["c_i"][:, p2, C - 1:C], s["rci"][:, k:k + 1],
                    None, ALU.mult))
                self.run("dv", lambda E_, k=k, p2=p2: E_.tensor_scalar(
                    s["t8b"][:, 0:1], s["c_r"][:, p2, C - 1:C], s["rci"][:, k:k + 1],
                    None, ALU.mult))
                self.run("dv", lambda E_, k=k, p2=p2, par=par: E_.scalar_tensor_tensor(
                    s["q_r"][:, 1 - par, k:k + 1], s["c_r"][:, p2, C - 1:C],
                    s["rcr"][:, k:k + 1], s["t8a"][:, 0:1], ALU.mult, ALU.add))
                self.run("dv", lambda E_, k=k, p2=p2, par=par: E_.scalar_tensor_tensor(
                    s["q_i"][:, 1 - par, k:k + 1], s["c_i"][:, p2, C - 1:C],
                    s["rcr"][:, k:k + 1], s["t8b"][:, 0:1], ALU.mult, ALU.subtract))
            if gi >= 1:
                pcc, pkk = groups[gi - 1]
                pp = (gi - 1) % 2
                pck = slice(pcc * C, (pcc + 1) * C)
                self.wait("dv", "sgc", m_ms[gi - 1])
                h_cnt[groups[gi - 1]] = self.inc(
                    "dv", lambda E_, pkk=pkk, pck=pck, pp=pp: E_.tensor_add(
                        s["zh"][:, pkk, pck], s["m_r"][:, pp, :], s["m_i"][:, pp, :]),
                    "sdve")

        self.w_att["wy"] = self.load_slot("ws0", self.dw["wy"])
        self.prefetch_gu((0, 0), 0, 0, slot="ps2")

        tc = self.tap("d_h", s["zh"][...], lambda dr, t=t: dr[:, :, t * T:(t + 1) * T], t)
        if tc:
            self.wait("dv", "sx", tc)

        qn = 512 // C
        rms_half_done = []
        for half in range((T + 511) // 512):
            bnk = 6 + half
            for k in range(KD):
                sq = ("szbb", "szb2b")[k % 2]
                self.need("ac", sq)
                for q4 in range(qn):
                    cc = half * qn + q4
                    self.wait("ac", "sdve", h_cnt[(cc, k)])
                    a = self.inc("ac", lambda E_, k=k, sq=sq, q4=q4, cc=cc: E_.activation(
                        out=s[sq][:, q4 * C:(q4 + 1) * C],
                        in_=s["zh"][:, k, cc * C:(cc + 1) * C],
                        func=AF.Square), "sact")
                if k == 0:
                    self.need("pe", f"bank{bnk}")
                self.wait("pe", "sact", a)
                p = self.inc("pe", lambda E_, sq=sq, bnk=bnk, k=k: E_.matmul(
                    pk[bnk][0:1, 0:512], s["onesc2"][...], s[sq][...],
                    start=(k == 0), stop=(k == KD - 1)), "spe")
                self.release(sq, "spe", p)
            rms_half_done.append(p)

        rb_done = []
        for half in range((T + 511) // 512):
            hw = min(T, 512)
            tok = slice(half * hw, (half + 1) * hw)
            bnk = 6 + half
            rv = s["rows"][0:1, S:S + hw]
            rvb = s["rows_b"][0:1, 0:hw]
            self.need("dv", "row_bufs")
            self.wait("dv", "spe", rms_half_done[half])
            d = self.inc("dv", lambda E_, rv=rv, bnk=bnk, hw=hw: E_.tensor_scalar(
                rv, pk[bnk][0:1, 0:hw], 1.0 / D, RMS_EPS, ALU.mult, ALU.add), "sdve")
            self.wait("ac", "sdve", d)
            self.run("ac", lambda E_, rv=rv: E_.activation(out=rv, in_=rv, func=AF.Ln))
            a = self.inc("ac", lambda E_, rv=rv, rvb=rvb: E_.activation(
                out=rvb, in_=rv, func=AF.Exp, scale=-0.5), "sact")
            self.need("pe", f"bank{bnk}")
            self.wait("pe", "sact", a)
            p = self.inc("pe", lambda E_, rvb=rvb, bnk=bnk, hw=hw: E_.matmul(
                pk[bnk][:, 0:hw], s["onesrb"][...], rvb, start=True, stop=True), "spe")
            self.release("row_bufs", "spe", p)
            self.wait("dv", "spe", p)
            d = self.inc("dv", lambda E_, tok=tok, bnk=bnk, hw=hw: E_.tensor_copy(
                s["rb"][:, tok], pk[bnk][:, 0:hw]), "sdve")
            self.release(f"bank{bnk}", "sdve", d)
            rb_done.append(d)

        ya_pe_done = self.c["spe"]
        first = True
        for sub in range(nsub):
            tok = slice(sub * S, (sub + 1) * S)
            for k in range(KD):
                if first:
                    self.wait("dv", "spe", ya_pe_done)
                    first = False
                d = self.inc("dv", lambda E_, k=k, tok=tok: E_.scalar_tensor_tensor(
                    s["xn"][:, k, tok], s["zh"][:, k, tok], s["nw"][:, k:k + 1],
                    s["rb"][:, tok], ALU.mult, ALU.mult), "sdve")
            self.ln_done[sub] = d

        for sub in range(nsub):
            self.proj_pass("wy", sub, (0, 1, 2, 3, 4, 5))
        self.free_slot("ws0", self.c["spe"])
        tc = self.tap("d_x1", s["x0"][...], lambda dr, t=t: dr[:, :, t * T:(t + 1) * T], t)
        if tc:
            self.wait("dv", "sx", tc)

        self.ln_done = [0] * nsub
        self.layernorm(s["x0"], want_f32=True, extra_dv=self.wy_done,
                       xb_guard=self.c["spe"])
        self.ffn_ln_done = list(self.ln_done)
        tc = self.tap("d_xn2", s["xn"][...], lambda dr, t=t: dr[:, :, t * T:(t + 1) * T], t)
        if tc:
            self.wait("dv", "sx", tc)

        self.need("pe", "bank6")
        free_ps0 = free_ps1 = 0
        self.wait("pe", "sdve", self.ln_done[(T // S) - 1])
        for k in range(KD):
            for g in range(ngrp):
                gtok = slice(g * 128, (g + 1) * 128)
                fn = (lambda E_, k=k, g=g, gtok=gtok: E_.matmul(
                    pk[6][:, g * 8:(g + 1) * 8], s["zh"][:, k, gtok], s["rwt"][:, k, :],
                    start=(k == 0 and g == 0), stop=(k == KD - 1 and g == ngrp - 1)))
                if (k == 3 or k == KD - 1) and g == ngrp - 1:
                    p = self.inc("pe", fn, "spe")
                    if k == 3:
                        free_ps0 = p
                    else:
                        free_ps1 = p
                else:
                    self.run("pe", fn)
        self.free_slot("ps0", free_ps0)
        self.free_slot("ps1", free_ps1)
        self.wait("dv", "spe", free_ps1)
        d = self.inc("dv", lambda E_: E_.tensor_copy(s["gt"][...], pk[6][:, 0:ngrp * 8]),
                     "sdve")
        self.release("bank6", "sdve", d)
        self.run("dv", lambda E_: E_.tensor_copy(s["padd"][...], s["t8c"][...]))
        self.run("dv", lambda E_: E_.tensor_copy(s["padd"][...], s["t8c"][...]))
        for g in range(ngrp):
            gt = s["gt"][:, g * 8:(g + 1) * 8]
            for e in range(E):
                self.run("dv", lambda E_, gt=gt, e=e, g=g: E_.tensor_scalar(
                    s["junk8"][...], gt, gt[:, e:e + 1], 0.0, ALU.is_gt, ALU.add,
                    accum_out=s["cnts"][:, g * 8 + e:g * 8 + e + 1]))
        for g in range(ngrp):
            self.run("dv", lambda E_, g=g: E_.reduce_max(
                s["mxs"][:, g:g + 1], s["gt"][:, g * 8:(g + 1) * 8], AX.X))
        for g in range(ngrp):
            self.run("dv", lambda E_, g=g: E_.tensor_scalar(
                s["mskb"][:, g * 8:(g + 1) * 8], s["cnts"][:, g * 8:(g + 1) * 8],
                1.5, None, ALU.is_lt))
        for g in range(ngrp):
            self.run("dv", lambda E_, g=g: E_.tensor_scalar(
                s["ex"][:, g * 8:(g + 1) * 8], s["gt"][:, g * 8:(g + 1) * 8],
                s["mxs"][:, g:g + 1], None, ALU.subtract))
        d = self.pad_inc("dv", "sdve")
        self.wait("ac", "sdve", d)
        self.run("ac", lambda E_: E_.activation(
            out=s["ex"][...], in_=s["ex"][...], func=AF.Exp))
        a = self.pad_inc("ac", "sact")
        self.wait("dv", "sact", a)
        for g in range(ngrp):
            self.run("dv", lambda E_, g=g: E_.reduce_sum(
                s["sms"][:, g:g + 1], s["ex"][:, g * 8:(g + 1) * 8], AX.X))
        self.run("dv", lambda E_: E_.tensor_copy(s["padd"][...], s["t8c"][...]))
        self.run("dv", lambda E_: E_.tensor_copy(s["padd"][...], s["t8c"][...]))
        self.run("dv", lambda E_: E_.reciprocal(s["mxs"][...], s["sms"][...]))
        self.run("dv", lambda E_: E_.tensor_copy(s["padd"][...], s["t8c"][...]))
        self.run("dv", lambda E_: E_.tensor_copy(s["padd"][...], s["t8c"][...]))
        for g in range(ngrp):
            self.run("dv", lambda E_, g=g: E_.scalar_tensor_tensor(
                s["ex"][:, g * 8:(g + 1) * 8], s["ex"][:, g * 8:(g + 1) * 8],
                s["mxs"][:, g:g + 1], s["mskb"][:, g * 8:(g + 1) * 8],
                ALU.mult, ALU.mult))
        self.topk_done = d
        self.ew_ngrp = ngrp
        ng8 = (T // 128) * 8
        tc = self.tap("d_gt", s["gt"][...], lambda dr, t=t, ng8=ng8: dr[:, t * ng8:(t + 1) * ng8], t)
        tc2 = self.tap("d_ew", s["ew_fm"][...], lambda dr, t=t: dr[:, t * T:(t + 1) * T], t)
        if tc2:
            self.wait("dv", "sx", tc2)
        router_pe = self.c["spe"]
        self.free_slot("ps0", router_pe)
        self.free_slot("ps1", router_pe)

        for e in range(E):
            self.expert(t, e)

        xl = []
        for sub in range(nsub):
            self.wait("sy", "sdve", self.pass2_sub_done[sub])
            self.nstores += 1
            st = self.dma("sy", self.dout[:, t * nsub + sub], s["x0"][:, sub], "sst")
            if t + 1 < self.ntiles:
                self.wait("sy", "sst", st)
                xl.append(("sx", self.dma(
                    "sy", s["x0"][:, sub], self.dx[:, (t + 1) * nsub + sub])))
        if t + 1 < self.ntiles:
            self.xload.append(xl)
            self.free_slot("ws0", self.c["spe"])
            self.free_slot("ws1", self.c["spe"])
            self.w_att["za"] = self.load_slot("ws0", self.dw["wza"])
            self.w_att["z"] = self.load_slot("ws1", self.dw["wz"])
            self.w_att["ya"] = self.load_slot("ps2", self.dw["wya"])


    def emit_ew_transpose(self):
        s, pk = self.s, self.pk
        ngrp = self.ew_ngrp
        d2 = 0
        for g in range(ngrp):
            if g % 4 == 0:
                self.need("pe", "bank7")
                self.wait("pe", "sdve", self.topk_done)
            p = self.inc("pe", lambda E_, g=g: E_.transpose(
                pk[7][0:8, (g % 4) * 128:(g % 4 + 1) * 128],
                s["ex"][:, g * 8:(g + 1) * 8], s["idn"][...]), "spe")
            if g % 4 == 3 or g == ngrp - 1:
                n0 = (g // 4) * 4
                self.wait("dv", "spe", p)
                d2 = self.inc("dv", lambda E_, n0=n0, g=g: E_.tensor_copy(
                    s["ew_fm"][:, n0 * 128:(g + 1) * 128],
                    pk[7][0:8, 0:(g + 1 - n0) * 128]), "sdve")
                self.release("bank7", "sdve", d2)
        self.ew_done = d2
    def ew_xe(self, e):
        s, pk = self.s, self.pk
        T = self.T
        if e == 0:
            self.emit_ew_transpose()
        self.need("gp", "ew_row")
        self.wait("gp", "sdve", self.ew_done)
        mv = self.dma("gp", s["ew_row"][...], s["ew_fm"][e:e + 1, :], "swg")
        for half in range((T + 511) // 512):
            hw = min(T, 512)
            tok = slice(half * hw, (half + 1) * hw)
            self.need("pe", "bank7")
            self.wait("pe", "swg", mv)
            p = self.inc("pe", lambda E_, tok=tok, hw=hw: E_.matmul(
                pk[7][:, 0:hw], s["onesrb"][...], s["ew_row"][0:1, tok],
                start=True, stop=True), "spe")
            self.wait("dv", "spe", p)
            self.need("dv", "ewb")
            d = self.inc("dv", lambda E_, tok=tok, hw=hw: E_.tensor_copy(
                s["ewb"][:, tok], pk[7][:, 0:hw]), "sdve")
            self.release("bank7", "sdve", d)
        self.release("ew_row", "spe", p)
        self.need("dv", "dgate")
        for k in range(KD):
            d = self.inc("dv", lambda E_, k=k: E_.tensor_mul(
                s["dgate"][:, k, :], s["xn"][:, k, :], s["ewb"][:, 0:T]), "sdve")
        self.release("ewb", "sdve", d)
        self.xe_done = d
        self.cur_xe = d

    def expert(self, t, e):
        s, pk = self.s, self.pk
        T, S = self.T, self.S
        nsub = T // S
        if e > 0:
            self.cur_xe = self.xe_done
        if not hasattr(self, "pass2_sub_done"):
            self.pass2_sub_done = [0] * nsub

        if e == 0:
            self.prefetch_gu((e, 1), e, 1, slot="ps0")
            self.prefetch_gu((e, 2), e, 2, slot="ps1")
        else:
            self.prefetch_gu((e, 1), e, 1)
            self.prefetch_gu((e, 2), e, 2)

        def emit_gate(j, l, sub):
            ic = j * 4 + l
            slot, v = self.gu_loaded[(e, j)]
            w = self.slot_t[slot]
            tok = slice(sub * S, (sub + 1) * S)
            bg = (ic % 3) * 2
            self.need("pe", f"bank{bg}")
            if l == 0 and sub == 0:
                self.wait("pe", "sdve", self.ffn_ln_done[sub])
                self.wait("pe", "swg", v)
            for k in range(KD):
                fn = (lambda E_, k=k, l=l, bg=bg, tok=tok, w=w: E_.matmul(
                    pk[bg][...], w[:, k, l * 128:(l + 1) * 128],
                    s["xn"][:, k, tok], start=(k == 0), stop=(k == KD - 1)))
                if k == KD - 1:
                    p = self.inc("pe", fn, "spe")
                else:
                    self.run("pe", fn)
            self.wait("ac", "spe", p)
            a = self.inc("ac", lambda E_, bg=bg, ic=ic, tok=tok: E_.activation(
                out=s["g"][:, ic, tok], in_=pk[bg][...], func=AF.Silu), "sact")
            self.release(f"bank{bg}", "sact", a)
            self.g_silu[(ic, sub)] = a

        def emit_up(j, l, sub):
            ic = j * 4 + l
            slot, v = self.gu_loaded[(e, j)]
            w = self.slot_t[slot]
            tok = slice(sub * S, (sub + 1) * S)
            bu = (ic % 3) * 2 + 1
            self.need("pe", f"bank{bu}")
            if l == 0 and sub == 0 and j in (0, 1) and e == 0:
                pass
            if (j, l, sub) == self.first_up:
                if e == 0:
                    self.ew_xe(0)
                self.wait("pe", "sdve", self.cur_xe)
            for k in range(KD):
                fn = (lambda E_, k=k, l=l, bu=bu, tok=tok, w=w: E_.matmul(
                    pk[bu][...], w[:, k, 512 + l * 128:512 + (l + 1) * 128],
                    s["dgate"][:, k, tok], start=(k == 0), stop=(k == KD - 1)))
                if k == KD - 1:
                    p = self.inc("pe", fn, "spe")
                else:
                    self.run("pe", fn)
            self.wait("dv", "spe", p)
            self.wait("dv", "sact", self.g_silu[(ic, sub)])
            d = self.inc("dv", lambda E_, ic=ic, bu=bu, tok=tok: E_.tensor_mul(
                s["g"][:, ic, tok], s["g"][:, ic, tok], pk[bu][...]), "sdve")
            self.release(f"bank{bu}", "sdve", d)
            self.last_g = d
            return p

        self.g_silu = {}
        self.last_g = 0
        slot_last = {}
        if e == 0:
            self.first_up = (0, 0, 0)
            order = ([("g", 0, l, 0) for l in range(4)] +
                     [("g", 1, l, 0) for l in range(4)] +
                     [("u", 0, l, 0) for l in range(4)] +
                     [("u", 1, l, 0) for l in range(4)] +
                     [(op, j, l, 1) for j in (0, 1) for l in range(4)
                      for op in ("g", "u")] +
                     [(op, j, l, sub) for j in (2, 3) for sub in range(nsub)
                      for l in range(4) for op in ("g", "u")])
            if nsub == 1:
                order = [o for o in order if o[3] == 0 or o[0] == "x"]
                order = ([("g", 0, l, 0) for l in range(4)] +
                         [("g", 1, l, 0) for l in range(4)] +
                         [("u", 0, l, 0) for l in range(4)] +
                         [("u", 1, l, 0) for l in range(4)] +
                         [(op, j, l, 0) for j in (2, 3)
                          for l in range(4) for op in ("g", "u")])
        else:
            self.first_up = (0, 0, 0)
            order = [(op, j, l, sub) for j in range(4) for sub in range(nsub)
                     for l in range(4) for op in ("g", "u")]
        w2q = 0
        for op, j, l, sub in order:
            if op == "g":
                emit_gate(j, l, sub)
            else:
                p = emit_up(j, l, sub)
                slot_last[j] = max(slot_last.get(j, 0), p)
                done_subs = sum(1 for (oo, jj, ll, ss) in order[:order.index((op, j, l, sub)) + 1]
                                if oo == "u" and jj == j)
                if done_subs == 4 * nsub:
                    self.free_slot(self.gu_loaded[(e, j)][0], slot_last[j])
                    if j == 0:
                        self.prefetch_gu((e, 3), e, 3)
                    elif j == 1:
                        self.prefetch_w2(e, 0)
                    elif j == 2:
                        self.prefetch_w2(e, 1)
        g_done = self.last_g
        pass1_pe = self.c["spe"]
        self.release("dgate", "spe", pass1_pe)
        self.xn_free_pe = pass1_pe
        if e + 1 < E:
            self.ew_xe(e + 1)

        sl0, v0 = self.w2_loaded[(e, 0)]
        sl1, v1 = self.w2_loaded[(e, 1)]
        w2a, w2b = self.slot_t[sl0], self.slot_t[sl1]
        last_p = 0
        for sub in range(nsub):
            tok = slice(sub * S, (sub + 1) * S)
            for m in range(KD):
                b = 4 + (m % 2)
                self.need("pe", f"bank{b}")
                if sub == 0 and m == 0:
                    self.wait("pe", "swg", max(v0, v1))
                    self.wait("pe", "sdve", g_done)
                for ic in range(ICN):
                    w2s = w2a if ic < 8 else w2b
                    icl = ic % 8
                    fn = (lambda E_, icl=icl, m=m, b=b, tok=tok, w2s=w2s, ic=ic: E_.matmul(
                        pk[b][...], w2s[:, icl, m * 128:(m + 1) * 128],
                        s["g"][:, ic, tok], start=(ic == 0), stop=(ic == ICN - 1)))
                    if ic == ICN - 1:
                        p = self.inc("pe", fn, "spe")
                    else:
                        self.run("pe", fn)
                last_p = p
                self.wait("dv", "spe", p)
                d = self.inc("dv", lambda E_, m=m, b=b, sub=sub:
                             E_.tensor_add(s["x0"][:, sub, m, :],
                                           s["x0"][:, sub, m, :],
                                           pk[b][...]), "sdve")
                self.release(f"bank{b}", "sdve", d)
            self.pass2_sub_done[sub] = d
        self.free_slot(sl0, last_p)
        self.free_slot(sl1, last_p)
        if e + 1 < E:
            self.prefetch_gu((e + 1, 0), e + 1, 0)


def _tables(C):
    d = np.arange(D, dtype=np.float64)
    th = 0.001 ** (d / (D - 1))
    j = np.arange(C, dtype=np.float64)[None, :]
    thj = th[:, None] * j
    cr = np.cos(thj)
    sr = np.sin(thj)
    f0 = np.exp(1j * th)
    rc = np.exp(1j * th * C)
    return cr, sr, f0.real, f0.imag, rc.real, rc.imag


def _pm2(v):
    return np.ascontiguousarray(np.asarray(v, np.float64).reshape(KD, 128).T,
                                dtype=np.float32)


def _pm3(m):
    return np.ascontiguousarray(
        np.asarray(m).reshape(KD, 128, m.shape[1]).transpose(1, 0, 2),
        dtype=np.float32)


def make_inputs(L, C, x_b, Wz, bz, Wza, bza, Wy, by, Wya, bya, nw, lcr, lci,
                w1, w2, rw):
    cr, sr, f0r, f0i, rcr, rci = _tables(C)
    com = {
        "wz": _pm3(np.ascontiguousarray(Wz.T)).reshape(128, KD * D),
        "wza": _pm3(np.ascontiguousarray(Wza.T)).reshape(128, KD * D),
        "wy": _pm3(np.ascontiguousarray(Wy.T)).reshape(128, KD * D),
        "wya": _pm3(np.ascontiguousarray(Wya.T)).reshape(128, KD * D),
        "bz": _pm2(bz), "bza": _pm2(bza), "by": _pm2(by), "bya": _pm2(bya),
        "nw": _pm2(nw), "lcr": _pm2(lcr), "lci": _pm2(lci),
        "f0r": _pm2(f0r), "f0i": _pm2(f0i),
        "rcr": _pm2(rcr), "rci": _pm2(rci),
        "cr": _pm3(cr), "sr": _pm3(sr),
        "rwt": _pm3(np.ascontiguousarray(rw.T)),
        "idn": np.eye(128, dtype=np.float32),
        "onesr": np.ones((1, 128), np.float32),
        "onesrb": np.ones((1, 128), ml_dtypes.bfloat16),
        "onesc": np.ones((128, 1), np.float32),
        "onesc2": np.ones((128, 1), ml_dtypes.bfloat16),
    }
    w1t = np.empty((E, 4, 128, KD * 1024), np.float32)
    for e_ in range(E):
        wt = np.ascontiguousarray(w1[e_].T)
        re = np.empty((D, I2), np.float32)
        for jj in range(4):
            re[:, jj * 1024:jj * 1024 + 512] = wt[:, jj * 512:(jj + 1) * 512]
            re[:, jj * 1024 + 512:(jj + 1) * 1024] = \
                wt[:, 2048 + jj * 512:2048 + (jj + 1) * 512]
        full = _pm3(re)
        for jj in range(4):
            w1t[e_, jj] = full[:, :, jj * 1024:(jj + 1) * 1024].reshape(128, -1)
    w2t = np.empty((E, 2, 128, 8 * 1024), np.float32)
    for e_ in range(E):
        full = np.ascontiguousarray(
            w2[e_].T.reshape(ICN, 128, D).transpose(1, 0, 2))
        for hh in range(2):
            w2t[e_, hh] = full[:, hh * 8:(hh + 1) * 8, :].reshape(128, -1)
    com["w1"] = w1t
    com["w2"] = w2t
    S = 512
    xs = [np.ascontiguousarray(
        x_b[b].T.reshape(KD, 128, L).transpose(1, 0, 2)
        .reshape(128, KD, L // S, S).transpose(0, 2, 1, 3), np.float32)
        for b in range(x_b.shape[0])]
    return com, xs


def run(x_full, Wz, bz, Wza, bza, Wy, by, Wya, bya, nw, lcr, lci, w1, w2, rw,
        T=1024, C=256, debug=False):
    B, L, _ = x_full.shape
    com, xs = make_inputs(L, C, x_full, Wz, bz, Wza, bza, Wy, by, Wya, bya,
                          nw, lcr, lci, w1, w2, rw)
    bld = Builder(L, T, C, debug=debug)
    nc = bld.build()
    in_maps = [dict(com, x0=xs[b]) for b in range(B)]
    res = run_bass_kernel_spmd(nc, in_maps, core_ids=list(range(B)))
    outs = []
    for b in range(B):
        o = np.asarray(res.results[b]["out"])
        o = o.transpose(0, 2, 1, 3).reshape(128, D // 128, L)
        outs.append(np.ascontiguousarray(
            o.transpose(1, 0, 2).reshape(D, L).T))
    if debug:
        return np.stack(outs), res.results
    return np.stack(outs)


def kernel(L0_attn_x, L0_attn_Wz, L0_attn_bz, L0_attn_Wza, L0_attn_bza,
           L0_attn_Wy, L0_attn_by, L0_attn_Wya, L0_attn_bya, L0_attn_norm_w,
           L0_attn_lc_re, L0_attn_lc_im, L0_ffn_w1, L0_ffn_w2, L0_ffn_router_w):
    x = np.asarray(L0_attn_x, np.float32)
    out = run(x,
              np.asarray(L0_attn_Wz, np.float32), np.asarray(L0_attn_bz, np.float32),
              np.asarray(L0_attn_Wza, np.float32), np.asarray(L0_attn_bza, np.float32),
              np.asarray(L0_attn_Wy, np.float32), np.asarray(L0_attn_by, np.float32),
              np.asarray(L0_attn_Wya, np.float32), np.asarray(L0_attn_bya, np.float32),
              np.asarray(L0_attn_norm_w, np.float32),
              np.asarray(L0_attn_lc_re, np.float32), np.asarray(L0_attn_lc_im, np.float32),
              np.asarray(L0_ffn_w1, np.float32), np.asarray(L0_ffn_w2, np.float32),
              np.asarray(L0_ffn_router_w, np.float32))
    return out.astype(np.float32)
